# version 17
# baseline (speedup 1.0000x reference)
import numpy as np
import ml_dtypes
from contextlib import ExitStack

import concourse.bass as bass
import concourse.mybir as mybir
from concourse.bass_utils import run_bass_kernel_spmd

F32 = mybir.dt.float32
BF16 = mybir.dt.bfloat16
AF = mybir.ActivationFunctionType
ALU = mybir.AluOpType
AX = mybir.AxisListType

B, T, D = 4, 4096, 2048
H, G, DH = 16, 4, 128
NE, HF, HS = 8, 1408, 2816
NDC = D // 128
HC = HF // 128
ROPE_BASE = 10000.0
SCALE = 1.0 / np.sqrt(DH)
EXP_BIAS = -8.0
EPS_LN = 1e-5
EPS_GATE = 1e-20
NSLOT = 4
YT = 4480
CPRIME = 896
MASK_NEG = -60000.0

DBG_TAPS = set()
P0_BLOCKS = 8

ENGINES = ("tensor", "vector", "scalar", "sync", "gpsimd")


class Prog:

    def __init__(self, nc):
        self.nc = nc
        self.q = {e: [] for e in ENGINES}
        self.sems = {}
        self.cnt = {}
        self.last_wait = {e: {} for e in ENGINES}
        self.bar_k = 0
        self.taps = {}

    def mksem(self, stack, *names):
        for n in names:
            self.sems[n] = stack.enter_context(self.nc.semaphore(n))
            self.cnt[n] = 0

    def mkdmasem(self, stack, eng, *names):
        if not hasattr(self, "dma_sems"):
            self.dma_sems = []
        for n in names:
            self.sems[n] = stack.enter_context(self.nc.semaphore(n))
            self.cnt[n] = 0
            self.dma_sems.append((eng, n))

    AUTO_SEM = {"vector": "ve", "scalar": "se"}

    def op(self, eng, build, inc=None, amt=1):
        auto = self.AUTO_SEM.get(eng)
        if auto is not None:
            self.wait(eng, auto, self.cnt[auto])
            assert inc is None or inc == auto
            inc = auto
        semh = self.sems[inc] if inc else None
        if inc:
            self.cnt[inc] += amt

        def go(E, build=build, semh=semh, amt=amt):
            ins = build(E)
            if semh is not None:
                ins.then_inc(semh, amt)
        self.q[eng].append(go)
        return self.cnt[inc] if inc else None

    def op_relaxed(self, eng, build, amt=1):
        auto = self.AUTO_SEM[eng]
        self.cnt[auto] += amt
        semh = self.sems[auto]

        def go(E, build=build, semh=semh, amt=amt):
            build(E).then_inc(semh, amt)
        self.q[eng].append(go)
        return self.cnt[auto]

    def wait(self, eng, sem, val=None):
        if val is None:
            val = self.cnt[sem]
        if val <= 0:
            return
        lw = self.last_wait[eng]
        if lw.get(sem, -1) >= val:
            return
        lw[sem] = val
        semh = self.sems[sem]
        self.q[eng].append(lambda E, s=semh, v=val: E.wait_ge(s, v))

    def dma(self, eng, out, in_, sem):
        semh = self.sems[sem]
        self.cnt[sem] += 16

        def go(E, out=out, in_=in_, semh=semh):
            E.dma_start(out=out, in_=in_).then_inc(semh, 16)
        self.q[eng].append(go)
        return self.cnt[sem]

    def barrier(self, bar_scratch=None):
        for e in ENGINES:
            for n in ("pe", "ve", "se", "po"):
                self.wait(e, n)
            for _eng, n in getattr(self, "dma_sems", []):
                self.wait(e, n)

    def tap(self, name, ap_fn, waits):
        if name not in DBG_TAPS:
            return
        shape = ap_fn("shape")
        dt = ap_fn("dtype")
        t = self.nc.dram_tensor("tap_" + name, shape, dt, kind="ExternalOutput").ap()
        self.taps[name] = True
        for sem, val in waits:
            self.wait("sync", sem, val)
        self.dma("sync", t, ap_fn("ap"), "dt")

    def finalize(self, block):
        block.gpsimd(lambda E: [f(E) for f in self.q["gpsimd"]])
        block.sync(lambda E: [f(E) for f in self.q["sync"]])
        block.tensor(lambda E: [f(E) for f in self.q["tensor"]])
        block.vector(lambda E: [f(E) for f in self.q["vector"]])
        block.scalar(lambda E: [f(E) for f in self.q["scalar"]])


def _mk_tap(tile, sl=None):
    def fn(what):
        ap = tile[:] if sl is None else sl
        if what == "shape":
            return list(ap.shape)
        if what == "dtype":
            return ap.dtype
        return ap
    return fn



def emit_transpose_tile(p, psT, ident_bf, src_tile, dst_view, tt, pe_state):
    for g4 in range(4):
        par = (tt * 4 + g4) % 2
        p.wait("tensor", "se", pe_state["psT_free"][par])
        for j in range(4):
            dc = g4 * 4 + j
            p.op("tensor",
                 lambda E, o=psT[par][:, j * 128:(j + 1) * 128],
                 i=src_tile[:, dc * 128:(dc + 1) * 128], idn=ident_bf[:]:
                 E.transpose(o, i, idn),
                 inc="pe" if j == 3 else None)
        done = p.cnt["pe"]
        p.wait("scalar", "pe", done)
        out = dst_view[:, g4 * 4:(g4 + 1) * 4, tt * 128:(tt + 1) * 128]
        src = psT[par][:].rearrange("p (j t) -> p j t", t=128)
        c = p.op("scalar", lambda E, o=out, i=src: E.copy(o, i), inc="se")
        pe_state["psT_free"][par] = c


def emit_rope(p, psQK, cs, cs_off, dst_lo, dst_hi, scr1, scr2, pe_count):
    p.wait("vector", "pe", pe_count)
    cosap = cs[0:64, cs_off:cs_off + 512]
    sinap = cs[64:128, cs_off:cs_off + 512]
    p.op("vector", lambda E, o=scr1[0:64, :], a=psQK[0:64, :], b=cosap:
         E.tensor_tensor(o, a, b, ALU.mult))
    p.op("vector", lambda E, o=scr2[0:64, :], a=psQK[64:128, :], b=sinap:
         E.tensor_tensor(o, a, b, ALU.mult))
    p.op("vector", lambda E, o=dst_lo, a=scr1[0:64, :], b=scr2[0:64, :]:
         E.tensor_tensor(o, a, b, ALU.subtract), inc="ve")
    p.op("vector", lambda E, o=scr1[0:64, :], a=psQK[0:64, :], b=sinap:
         E.tensor_tensor(o, a, b, ALU.mult))
    p.op("vector", lambda E, o=scr2[0:64, :], a=psQK[64:128, :], b=cosap:
         E.tensor_tensor(o, a, b, ALU.mult))
    return p.op("vector", lambda E, o=dst_hi, a=scr1[0:64, :], b=scr2[0:64, :]:
                E.tensor_tensor(o, a, b, ALU.add), inc="ve")


def emit_ln_stats_normalize(p, y_in, y_out, st, inv_n, junk):
    n = y_in.shape[-1]
    p.op("vector", lambda E, o=st["sum"][:], i=y_in: E.reduce_sum(o, i, axis=AX.X))
    p.op("vector", lambda E, o=junk, i=y_in, a=st["sumsq"][:]:
         E.scalar_tensor_tensor(o, i, 1.0, i, ALU.mult, ALU.mult, accum_out=a))
    p.op("vector", lambda E, o=st["mean"][:], i=st["sum"][:]:
         E.tensor_scalar(o, i, inv_n, None, ALU.mult))
    p.op("vector", lambda E, o=st["t1"][:], i=st["sumsq"][:]:
         E.tensor_scalar(o, i, inv_n, None, ALU.mult))
    p.op("vector", lambda E, o=st["m2"][:], a=st["mean"][:]:
         E.tensor_tensor(o, a, a, ALU.mult))
    p.op("vector", lambda E, o=st["t1"][:], a=st["t1"][:], b=st["m2"][:]:
         E.tensor_tensor(o, a, b, ALU.subtract))
    p.op("vector", lambda E, o=st["t1"][:], i=st["t1"][:]:
         E.tensor_scalar(o, i, EPS_LN, None, ALU.add))
    c = p.op("vector", lambda E, o=st["rv"][:], i=st["t1"][:]:
             E.reciprocal(o, i), inc="ve")
    p.wait("scalar", "ve", c)
    cs = p.op("scalar", lambda E, o=st["rstd"][:], i=st["rv"][:]:
              E.activation(out=o, in_=i, func=AF.Sqrt), inc="se")
    p.wait("vector", "se", cs)
    return p.op("vector", lambda E, o=y_out, i=y_in, m=st["mean"][:], r=st["rstd"][:]:
                E.tensor_scalar(o, i, m, r, ALU.subtract, ALU.mult), inc="ve")



def build_program(phase_limit=4):
    nc = bass.Bass("TRN2", target_bir_lowering=False)

    xf = nc.dram_tensor("xf", [T, D], F32, kind="ExternalInput").ap()
    xo = nc.dram_tensor("xo", [NSLOT * 512, D], F32, kind="ExternalInput").ap()
    etab = nc.dram_tensor("etab", [H, 128, YT], BF16, kind="ExternalInput").ap()
    wq_p = nc.dram_tensor("wq_p", [128, NDC, 16, 128], F32, kind="ExternalInput").ap()
    wk_p = nc.dram_tensor("wk_p", [128, NDC, 512], F32, kind="ExternalInput").ap()
    wv_p = nc.dram_tensor("wv_p", [128, NDC, 512], F32, kind="ExternalInput").ap()
    wo_p = nc.dram_tensor("wo_p", [128, NDC, D], F32, kind="ExternalInput").ap()
    rw_p = nc.dram_tensor("rw_p", [128, NDC, 8], F32, kind="ExternalInput").ap()
    wg_p = nc.dram_tensor("wg_p", [NE, HC, 128, NDC, 128], F32, kind="ExternalInput").ap()
    wu_p = nc.dram_tensor("wu_p", [NE, HC, 128, NDC, 128], F32, kind="ExternalInput").ap()
    wd_r = nc.dram_tensor("wd_r", [NE, HC, 128, D], F32, kind="ExternalInput").ap()
    sg_p = nc.dram_tensor("sg_p", [2 * HC, 128, NDC, 128], F32, kind="ExternalInput").ap()
    su_p = nc.dram_tensor("su_p", [2 * HC, 128, NDC, 128], F32, kind="ExternalInput").ap()
    sd_r = nc.dram_tensor("sd_r", [2 * HC, 128, D], F32, kind="ExternalInput").ap()
    csf = nc.dram_tensor("csf", [128, T], F32, kind="ExternalInput").ap()
    cso_in = nc.dram_tensor("cso", [128, NSLOT * 512], F32, kind="ExternalInput").ap()
    ident_in = nc.dram_tensor("ident", [128, 128], F32, kind="ExternalInput").ap()
    out_d = nc.dram_tensor("out", [NSLOT * 512, D], F32, kind="ExternalOutput").ap()

    attnT_park = nc.dram_tensor("attnT_park", [NSLOT, 128, H * 512], BF16, kind="Internal").ap()
    y1_park = nc.dram_tensor("y1_park", [NSLOT * 512, D], F32, kind="Internal").ap()
    y1T_park = nc.dram_tensor("y1T_park", [NSLOT, 128, NDC * 512], BF16, kind="Internal").ap()
    wq_bf = nc.dram_tensor("wq_bf", [128, NDC, 16, 128], BF16, kind="Internal").ap()
    wo_bf = nc.dram_tensor("wo_bf", [128, NDC, D], BF16, kind="Internal").ap()

    with ExitStack() as g:
        e = g.enter_context
        p = Prog(nc)
        p.mksem(g, "pe", "ve", "se", "po")
        p.mkdmasem(g, "gpsimd", "ldg", "dkv", "dxs", "dx0", "dx1", "dq0", "dq1",
                   "dg0", "dg1", "du0", "du1", "dwd0", "dwd1")
        p.mkdmasem(g, "sync", "lds", "det", "dxr0", "dxr1", "dat", "dpark", "dy1t", "dy0", "dy1", "do0", "do1", "dt")
        p.mkdmasem(g, "scalar", "dwi", "dwo")
        p.mkdmasem(g, "gpsimd", "dtg")

        ident_bf = e(nc.sbuf_tensor("ident_bf", [128, 128], BF16))
        ones_sq = e(nc.sbuf_tensor("ones_sq", [128, 128], BF16))
        expbias = e(nc.sbuf_tensor("expbias", [128, 1], F32))
        rw_sb = e(nc.sbuf_tensor("rw_sb", [128, NDC * 8], BF16))
        bar_scratch = None
        zeros_c = e(nc.sbuf_tensor("zeros_c", [128, 1], F32))
        ones_c = e(nc.sbuf_tensor("ones_c", [128, 1], F32))
        nc.const_aps.register(F32, 0.0, zeros_c[:])
        nc.const_aps.register(F32, 1.0, ones_c[:])
        p.op("vector", lambda E: E.memset(zeros_c[:], 0.0))
        p.op("vector", lambda E: E.memset(ones_c[:], 1.0))

        p.dma("gpsimd", ident_bf[:], ident_in, "ldg")
        p.dma("gpsimd", rw_sb[:], rw_p.rearrange("p dc e -> p (dc e)"), "ldg")
        p.op("vector", lambda E: E.memset(ones_sq[:], 1.0))
        p.op("vector", lambda E: E.memset(expbias[:], EXP_BIAS), inc="ve")
        consts_ve = p.cnt["ve"]
        p.wait("scalar", "ve", consts_ve)

        with ExitStack() as kv_era:
            ek = kv_era.enter_context
            K_sb = ek(nc.sbuf_tensor("K_sb", [128, G * T], BF16))
            V_sb = ek(nc.sbuf_tensor("V_sb", [128, (T // 128) * 512], BF16))
            Kv = K_sb[:].rearrange("p (g t) -> p g t", g=G)
            Vv = V_sb[:].rearrange("p (kc t) -> p kc t", t=512)

            with ExitStack() as s0:
                e0 = s0.enter_context
                csf_sb = e0(nc.sbuf_tensor("csf_sb", [128, T], F32))
                p.dma("sync", csf_sb[:], csf, "lds")
                wk_sb = e0(nc.sbuf_tensor("wk_sb", [128, NDC * 512], BF16))
                wv_sb = e0(nc.sbuf_tensor("wv_sb", [128, NDC * 512], BF16))
                x_st = [e0(nc.sbuf_tensor(f"x_st{i}", [128, 4 * D], BF16)) for i in range(2)]
                xT_blk = e0(nc.sbuf_tensor("xT_blk", [128, NDC * 512], BF16))
                rsc1 = e0(nc.sbuf_tensor("rsc1", [64, 512], F32))
                rsc2 = e0(nc.sbuf_tensor("rsc2", [64, 512], F32))
                psT = [e0(nc.psum_tensor(f"psT{i}", [128, 512], BF16)) for i in range(2)]
                psK = [e0(nc.psum_tensor(f"psK{i}", [128, 512], F32)) for i in range(4)]
                psV = [e0(nc.psum_tensor(f"psV{i}", [128, 512], F32)) for i in range(2)]

                p.dma("gpsimd", wk_sb[:].rearrange("p (dc c) -> p dc c", c=512), wk_p, "dkv")
                p.dma("gpsimd", wv_sb[:].rearrange("p (dc c) -> p dc c", c=512), wv_p, "dkv")
                w_dkv = p.cnt["dkv"]

                wqc_st = [e0(nc.sbuf_tensor("wqc_st0", [128, 8192], BF16))]
                wcast_out_at = [0]

                def _emit_cast(qq):
                    cpar = 0
                    if qq < 4:
                        src = wq_p[:, :, qq * 4:(qq + 1) * 4, :]
                        dstv = wqc_st[cpar][:].rearrange("p (dc h j) -> p dc h j", h=4, j=128)
                        dst = wq_bf[:, :, qq * 4:(qq + 1) * 4, :]
                    else:
                        q4 = qq - 4
                        src = wo_p[:, :, q4 * 512:(q4 + 1) * 512]
                        dstv = wqc_st[cpar][:].rearrange("p (dc c) -> p dc c", c=512)
                        dst = wo_bf[:, :, q4 * 512:(q4 + 1) * 512]
                    p.wait("gpsimd", "dwo", wcast_out_at[cpar])
                    at = p.dma("gpsimd", dstv, src, "dwi")
                    p.wait("scalar", "dwi", at)
                    wcast_out_at[cpar] = p.dma("scalar", dst, dstv, "dwo")
                wkv = wk_sb[:].rearrange("p (dc c) -> p dc c", c=512)
                wvv = wv_sb[:].rearrange("p (dc c) -> p dc c", c=512)
                xTv = xT_blk[:].rearrange("p (dc t) -> p dc t", t=512)

                p.wait("vector", "lds", p.cnt["lds"])
                p.wait("tensor", "ldg", p.cnt["ldg"])
                pe_state = {"psT_free": [0, 0]}
                x_free = [0, 0]
                xT_free = 0
                psK_free = [0, 0, 0, 0]
                psV_free = [0, 0]
                xs_at = [0, 0]

                def _p0_stage(bb):
                    par = bb % 2
                    p.wait("gpsimd", "pe", x_free[par])
                    at = 0
                    for tt in range(4):
                        tg = bb * 4 + tt
                        at = p.dma("gpsimd", x_st[par][:, tt * D:(tt + 1) * D],
                                   xf[tg * 128:(tg + 1) * 128, :], f"dx{par}")
                    xs_at[par] = at

                if P0_BLOCKS:
                    _p0_stage(0)
                for blk in range(P0_BLOCKS):
                    par = blk % 2
                    if blk + 1 < P0_BLOCKS:
                        _p0_stage(blk + 1)
                    _emit_cast(blk)
                    p.wait("tensor", f"dx{par}", xs_at[par])
                    p.wait("tensor", "pe", xT_free)
                    for tt in range(4):
                        emit_transpose_tile(p, psT, ident_bf,
                                            x_st[par][:, tt * D:(tt + 1) * D],
                                            xTv, tt, pe_state)
                    x_free[par] = p.cnt["pe"]
                    p.wait("tensor", "se", p.cnt["se"])
                    p.wait("tensor", "dkv", w_dkv)
                    for gi in range(G):
                        kpar = gi % 4
                        p.wait("tensor", "ve", psK_free[kpar])
                        for dc in range(NDC):
                            p.op("tensor",
                                 lambda E, o=psK[kpar][:], l=wkv[:, dc, gi * 128:(gi + 1) * 128],
                                 r=xTv[:, dc, :], s=(dc == 0), t=(dc == NDC - 1):
                                 E.matmul(o, l, r, start=s, stop=t),
                                 inc="pe" if dc == NDC - 1 else None)
                        psK_free[kpar] = emit_rope(
                            p, psK[kpar], csf_sb[:], blk * 512,
                            Kv[0:64, gi, blk * 512:(blk + 1) * 512],
                            Kv[64:128, gi, blk * 512:(blk + 1) * 512],
                            rsc1, rsc2, p.cnt["pe"])
                    for ts4 in range(4):
                        kc = blk * 4 + ts4
                        vpar = ts4 % 2
                        p.wait("tensor", "ve", psV_free[vpar])
                        for dc in range(NDC):
                            p.op("tensor",
                                 lambda E, o=psV[vpar][:], l=xTv[:, dc, ts4 * 128:(ts4 + 1) * 128],
                                 r=wvv[:, dc, :], s=(dc == 0), t=(dc == NDC - 1):
                                 E.matmul(o, l, r, start=s, stop=t),
                                 inc="pe" if dc == NDC - 1 else None)
                        p.wait("vector", "pe", p.cnt["pe"])
                        psV_free[vpar] = p.op("vector",
                                              lambda E, o=Vv[:, kc, :], i=psV[vpar][:]:
                                              E.tensor_copy(o, i), inc="ve")
                    xT_free = p.cnt["pe"]

                p.tap("K_sb", _mk_tap(K_sb), [("ve", p.cnt["ve"])])
                p.tap("V_sb", _mk_tap(V_sb), [("ve", p.cnt["ve"])])
                p.barrier(bar_scratch)

            if phase_limit < 1:
                _emit_dummy_out(p, nc, out_d, bar_scratch)
                p.finalize(e(nc.Block()))
                return nc, p

            E_h = [ek(nc.sbuf_tensor(f"E_h{i}", [128, YT], BF16)) for i in range(4)]
            eh_war = [0, 0, 0, 0]
            eh_at = [0, 0, 0, 0]
            cso_sb = ek(nc.sbuf_tensor("cso_sb", [128, NSLOT * 512], F32))
            p.dma("sync", cso_sb[:], cso_in, "lds")
            p1_lds = p.cnt["lds"]

            x_stage = ek(nc.sbuf_tensor("x_stage", [128, 4 * D], BF16))
            xstage_free = 0

            def _prefetch_xstage(sl):
                p.wait("gpsimd", "pe", xstage_free)
                at = 0
                for tt in range(4):
                    at = p.dma("gpsimd", x_stage[:, tt * D:(tt + 1) * D],
                               xo[(sl * 4 + tt) * 128:(sl * 4 + tt + 1) * 128, :],
                               "dxs")
                return at

            xs_at = _prefetch_xstage(0)

            for slot in range(NSLOT):
                with ExitStack() as s1:
                    e1 = s1.enter_context
                    Q_all = e1(nc.sbuf_tensor(f"Q_all{slot}", [128, H * 512], BF16))
                    attnT = e1(nc.sbuf_tensor(f"attnT{slot}", [128, H * 512], BF16))
                    with ExitStack() as s1b:
                        e1b = s1b.enter_context
                        xT_e = e1b(nc.sbuf_tensor(f"xT_e{slot}", [128, NDC * 512], BF16))
                        wq_sb = [e1b(nc.sbuf_tensor(f"wq_sb{slot}_{i}", [128, NDC * 512], BF16)) for i in range(2)]
                        rsc1 = e1b(nc.sbuf_tensor(f"rsc1b{slot}", [64, 512], F32))
                        rsc2 = e1b(nc.sbuf_tensor(f"rsc2b{slot}", [64, 512], F32))
                        psT = [e1b(nc.psum_tensor(f"psT1{slot}_{i}", [128, 512], BF16)) for i in range(2)]
                        psQ = [e1b(nc.psum_tensor(f"psQ{slot}_{i}", [128, 512], F32)) for i in range(4)]

                        xTv = xT_e[:].rearrange("p (dc t) -> p dc t", t=512)
                        pe_state = {"psT_free": [0, 0]}
                        p.wait("tensor", "dxs", xs_at)
                        for tt in range(4):
                            emit_transpose_tile(p, psT, ident_bf,
                                                x_stage[:, tt * D:(tt + 1) * D],
                                                xTv, tt, pe_state)
                        xstage_free = p.cnt["pe"]
                        p.wait("tensor", "se", p.cnt["se"])
                        if slot + 1 < NSLOT:
                            xs_at = _prefetch_xstage(slot + 1)
                        p.wait("vector", "lds", p1_lds)

                        wq_free = [0, 0]
                        wq_at = [0, 0]
                        psQ_free = [0, 0, 0, 0]

                        def _wq_dma(qtr):
                            par = qtr % 2
                            dstv = wq_sb[par][:].rearrange("p (dc h j) -> p dc h j",
                                                           h=4, j=128)
                            p.wait("gpsimd", "pe", wq_free[par])
                            p.wait("scalar", "pe", wq_free[par])
                            p.dma("gpsimd", dstv[:, 0:8, :, :],
                                  wq_bf[:, 0:8, qtr * 4:(qtr + 1) * 4, :], f"dq{par}")
                            wq_at[par] = p.dma(
                                "scalar", dstv[:, 8:16, :, :],
                                wq_bf[:, 8:16, qtr * 4:(qtr + 1) * 4, :], f"dq{par}")
                        _wq_dma(0)
                        _wq_dma(1)
                        for qtr in range(4):
                            par = qtr % 2
                            p.wait("tensor", f"dq{par}", wq_at[par])
                            wqv = wq_sb[par][:].rearrange("p (dc c) -> p dc c", c=512)
                            for hh in range(4):
                                h = qtr * 4 + hh
                                qpar = h % 4
                                p.wait("tensor", "ve", psQ_free[qpar])
                                for dc in range(NDC):
                                    p.op("tensor",
                                         lambda E, o=psQ[qpar][:], l=wqv[:, dc, hh * 128:(hh + 1) * 128],
                                         r=xTv[:, dc, :], s=(dc == 0), t=(dc == NDC - 1):
                                         E.matmul(o, l, r, start=s, stop=t),
                                         inc="pe" if dc == NDC - 1 else None)
                                psQ_free[qpar] = emit_rope(
                                    p, psQ[qpar], cso_sb[:], slot * 512,
                                    Q_all[0:64, h * 512:(h + 1) * 512],
                                    Q_all[64:128, h * 512:(h + 1) * 512],
                                    rsc1, rsc2, p.cnt["pe"])
                            if qtr < 2:
                                wq_free[par] = p.cnt["pe"]
                                _wq_dma(qtr + 2)
                        p.wait("tensor", "ve", p.cnt["ve"])
                        p.tap(f"Q_{slot}", _mk_tap(Q_all), [("ve", p.cnt["ve"])])

                    NB = 4
                    with ExitStack() as s1c:
                        e1c = s1c.enter_context
                        ew = 1408 + 1024 * slot
                        expt = [e1c(nc.sbuf_tensor(f"expt{slot}_{i}", [128, 512], BF16)) for i in range(NB)]
                        expw = [e1c(nc.sbuf_tensor(f"expw{slot}_{i}", [128, 512], BF16)) for i in range(NB)]
                        rb_sb = e1c(nc.sbuf_tensor(f"rb_sb{slot}", [128, 512], F32))
                        psS = [e1c(nc.psum_tensor(f"psS{slot}_{i}", [128, 512], F32)) for i in range(NB)]
                        psSum = [e1c(nc.psum_tensor(f"psSum{slot}_{i}", [128, 512], F32)) for i in range(2)]
                        psAV = [e1c(nc.psum_tensor(f"psAV{slot}_{i}", [128, 512], F32)) for i in range(2)]

                        nk = 8 * slot + 8
                        LAG = 3
                        norm_done = [p.cnt["ve"], p.cnt["ve"]]
                        pending = [None]

                        def _eh_load(h):
                            epar = h % 4
                            half = (ew // 1024) * 512 + 512
                            p.wait("sync", "ve", eh_war[epar])
                            p.wait("gpsimd", "ve", eh_war[epar])
                            a1 = p.dma("sync", E_h[epar][:, 0:half],
                                       etab[h][:, 0:half], "det")
                            a2 = p.dma("gpsimd", E_h[epar][:, half:ew],
                                       etab[h][:, half:ew], "dtg")
                            eh_at[epar] = (a1, a2)
                        for hh0 in range(4):
                            _eh_load(hh0)

                        def _normalize(h, mm_last):
                            hp = h % 2
                            p.wait("vector", "pe", mm_last)
                            p.op("vector", lambda E, o=rb_sb[:], i=psSum[hp][:]:
                                 E.reciprocal(o, i))
                            nd = p.op(
                                "vector",
                                lambda E, o=attnT[:, h * 512:(h + 1) * 512],
                                a=psAV[hp][:], b=rb_sb[:]:
                                E.tensor_tensor(o, a, b, ALU.mult), inc="ve")
                            norm_done[hp] = nd
                            return nd

                        for h in range(H):
                            gi = h // 4
                            epar = h % 4
                            hp = h % 2
                            p.wait("vector", "det", eh_at[epar][0])
                            p.wait("vector", "dtg", eh_at[epar][1])
                            exp_done = [p.cnt["se"]] * NB
                            tt_done = [p.cnt["ve"]] * NB
                            mm_done = [p.cnt["pe"]] * NB
                            sc_done = [0] * NB
                            for step in range(nk + LAG):
                                kc = step
                                if kc < nk:
                                    bpar = kc % NB
                                    p.wait("tensor", "se", exp_done[bpar])
                                    sc_done[bpar] = p.op(
                                        "tensor",
                                        lambda E, o=psS[bpar][:],
                                        l=Kv[:, gi, kc * 128:(kc + 1) * 128],
                                        r=Q_all[:, h * 512:(h + 1) * 512]:
                                        E.matmul(o, l, r, start=True, stop=True),
                                        inc="pe")
                                if kc == LAG - 1 and pending[0] is not None:
                                    norm_prev = pending[0]
                                    pending[0] = None
                                    norm_prev()
                                j = step - LAG
                                if j < 0:
                                    continue
                                bpar = j % NB
                                p.wait("scalar", "pe", sc_done[bpar])
                                p.wait("scalar", "ve", tt_done[bpar])
                                exp_done[bpar] = p.op_relaxed(
                                    "scalar",
                                    lambda E, o=expt[bpar][:], i=psS[bpar][:]:
                                    E.activation(out=o, in_=i, func=AF.Exp,
                                                 bias=expbias[:], scale=SCALE))
                                y0 = CPRIME + slot * 1024 - j * 128
                                p.wait("vector", "se", exp_done[bpar])
                                p.wait("vector", "pe", mm_done[bpar])
                                tt_done[bpar] = p.op_relaxed(
                                    "vector",
                                    lambda E, o=expw[bpar][:], a=expt[bpar][:],
                                    b=E_h[epar][:, y0:y0 + 512]:
                                    E.tensor_tensor(o, a, b, ALU.mult))
                                p.wait("tensor", "ve", tt_done[bpar])
                                if j == 0:
                                    p.wait("tensor", "ve", norm_done[hp])
                                p.op("tensor",
                                     lambda E, o=psSum[hp][:], l=ones_sq[:], r=expw[bpar][:],
                                     s=(j == 0), t=(j == nk - 1):
                                     E.matmul(o, l, r, start=s, stop=t))
                                mm_done[bpar] = p.op(
                                    "tensor",
                                    lambda E, o=psAV[hp][:], l=Vv[:, j, gi * 128:(gi + 1) * 128],
                                    r=expw[bpar][:], s=(j == 0), t=(j == nk - 1):
                                    E.matmul(o, l, r, start=s, stop=t),
                                    inc="pe")
                            mm_last = mm_done[(nk - 1) % NB]
                            eh_war[epar] = p.cnt["ve"]
                            if h + 4 < H:
                                _eh_load(h + 4)
                            pending[0] = (lambda hh=h, ml=mm_last: _normalize(hh, ml))
                        pending[0]()
                        pending[0] = None
                        p.wait("scalar", "ve", p.cnt["ve"])
                        p.dma("scalar", attnT_park[slot][:, 0:H * 256],
                              attnT[:, 0:H * 256], "dpark")
                        p.dma("scalar", attnT_park[slot][:, H * 256:H * 512],
                              attnT[:, H * 256:H * 512], "dpark")
                        p.tap(f"attnT_{slot}", _mk_tap(attnT), [("ve", p.cnt["ve"])])
                    p.barrier(bar_scratch)

        if phase_limit < 2:
            _emit_dummy_out(p, nc, out_d, bar_scratch)
            p.finalize(e(nc.Block()))
            return nc, p

        TB = 1024
        y1T_blk = e(nc.sbuf_tensor("y1T_blk", [128, NDC * TB], BF16))
        with ExitStack() as s2:
            e2 = s2.enter_context
            wo_sb = e2(nc.sbuf_tensor("wo_sb", [128, NDC * D], BF16))
            attnT_e = [e2(nc.sbuf_tensor(f"attnT_e{i}", [128, H * 512], BF16)) for i in range(2)]
            xres = [e2(nc.sbuf_tensor(f"xres{i}", [128, D], F32)) for i in range(2)]
            y1pre = [e2(nc.sbuf_tensor(f"y1pre{i}", [128, D], F32)) for i in range(2)]
            y1f = [e2(nc.sbuf_tensor(f"y1f{i}", [128, D], F32)) for i in range(2)]
            y1bf = [e2(nc.sbuf_tensor(f"y1bf{i}", [128, D], BF16)) for i in range(2)]
            y1T_e = e2(nc.sbuf_tensor("y1T_e", [128, NDC * 512], BF16))
            st = {
                "sum": e2(nc.sbuf_tensor("st_sum", [128, 1], F32)),
                "sumsq": e2(nc.sbuf_tensor("st_sumsq", [128, 1], F32)),
                "mean": e2(nc.sbuf_tensor("st_mean", [128, 1], F32)),
                "t1": e2(nc.sbuf_tensor("st_t1", [128, 1], F32)),
                "m2": e2(nc.sbuf_tensor("st_m2", [128, 1], F32)),
                "rv": e2(nc.sbuf_tensor("st_rv", [128, 1], F32)),
                "rstd": e2(nc.sbuf_tensor("st_rstd", [128, 1], F32)),
            }
            psY = [e2(nc.psum_tensor(f"psY{i}", [128, 512], F32)) for i in range(4)]
            psT = [e2(nc.psum_tensor(f"psT2{i}", [128, 512], BF16)) for i in range(2)]

            wov = wo_sb[:].rearrange("p (dc c) -> p dc c", c=D)
            for wi in range(8):
                p.dma("gpsimd", wov[:, wi * 2:(wi + 1) * 2, :],
                      wo_bf[:, wi * 2:(wi + 1) * 2, :], "ldg")
            wo_ldg = p.cnt["ldg"]
            xres_free = [0, 0]
            psY_free = 0
            y1T_parked = [0]
            y1_park_at = [0, 0]
            y1_tr_done = [0, 0]
            pe_state = {"psT_free": [0, 0]}
            aT_at = [0, 0]
            aT_free = [0, 0]

            def _at_load(slot):
                par = slot % 2
                p.wait("gpsimd", "pe", aT_free[par])
                at = 0
                for st in range(8):
                    at = p.dma("gpsimd",
                               attnT_e[par][:, st * 1024:(st + 1) * 1024],
                               attnT_park[slot][:, st * 1024:(st + 1) * 1024],
                               "dat")
                aT_at[par] = at
            pending_tr = [None]

            def _flush_tr():
                if pending_tr[0] is not None:
                    fn = pending_tr[0]
                    pending_tr[0] = None
                    fn()

            _at_load(0)
            for slot in range(NSLOT):
                par2 = slot % 2
                if slot + 1 < NSLOT:
                    _at_load(slot + 1)
                aTv = attnT_e[par2][:].rearrange("p (dc t) -> p dc t", t=512)
                if slot < 2:
                    y1Tv = (y1T_blk[:].rearrange("p (dc t) -> p dc t", t=TB)
                            [:, :, slot * 512:(slot + 1) * 512])
                else:
                    y1Tv = y1T_e[:].rearrange("p (dc t) -> p dc t", t=512)
                p.wait("tensor", "dat", aT_at[par2])
                p.wait("tensor", "ldg", wo_ldg)
                for ts in range(4):
                    par = ts % 2
                    p.wait("sync", "ve", xres_free[par])
                    xr_at = p.dma("sync", xres[par][:],
                                  xo[(slot * 4 + ts) * 128:(slot * 4 + ts + 1) * 128, :],
                                  f"dxr{par}")
                    p.wait("tensor", "ve", psY_free)
                    for dc in range(NDC):
                        for do in range(4):
                            p.op("tensor",
                                 lambda E, o=psY[do][:], l=aTv[:, dc, ts * 128:(ts + 1) * 128],
                                 r=wov[:, dc, do * 512:(do + 1) * 512],
                                 s=(dc == 0), t=(dc == NDC - 1):
                                 E.matmul(o, l, r, start=s, stop=t),
                                 inc="pe" if (dc == NDC - 1 and do == 3) else None)
                    mmc = p.cnt["pe"]
                    _flush_tr()
                    p.wait("vector", "pe", mmc)
                    p.wait("vector", f"dxr{par}", xr_at)
                    p.wait("vector", f"dy{par}", y1_park_at[par])
                    p.wait("vector", "pe", y1_tr_done[par])
                    for do in range(4):
                        p.op("vector",
                             lambda E, o=y1pre[par][:, do * 512:(do + 1) * 512],
                             a=psY[do][:],
                             b=xres[par][:, do * 512:(do + 1) * 512]:
                             E.tensor_tensor(o, a, b, ALU.add),
                             inc="ve" if do == 3 else None)
                    psY_free = p.cnt["ve"]
                    xres_free[par] = psY_free
                    c = emit_ln_stats_normalize(p, y1pre[par][:], y1f[par][:], st,
                                                1.0 / D, junk=y1f[par][:])
                    cbf = p.op("vector", lambda E, o=y1bf[par][:], i=y1f[par][:]:
                               E.tensor_copy(o, i), inc="ve")
                    p.wait("scalar", "ve", c)
                    y1_park_at[par] = p.dma(
                        "scalar", y1_park[(slot * 4 + ts) * 128:(slot * 4 + ts + 1) * 128, :],
                        y1f[par][:], f"dy{par}")

                    def _tr(par=par, ts=ts, sl=slot, cbf=cbf, yv=y1Tv):
                        p.wait("tensor", "ve", cbf)
                        if sl >= 2:
                            p.wait("tensor", "dy1t", y1T_parked[0])
                        emit_transpose_tile(p, psT, ident_bf, y1bf[par], yv, ts,
                                            pe_state)
                        y1_tr_done[par] = p.cnt["pe"]
                        if ts == 3 and sl >= 2:
                            p.wait("gpsimd", "se", p.cnt["se"])
                            p.wait("gpsimd", "pe", p.cnt["pe"])
                            pv = y1T_park[sl].rearrange("p (dc t) -> p dc t", t=512)
                            at = 0
                            for q4 in range(4):
                                at = p.dma("gpsimd", pv[:, q4 * 4:(q4 + 1) * 4, :],
                                           yv[:, q4 * 4:(q4 + 1) * 4, :], "dy1t")
                            y1T_parked[0] = at
                    pending_tr[0] = _tr
                aT_free[par2] = p.cnt["pe"]
            _flush_tr()
            p.barrier(bar_scratch)

        if phase_limit < 3:
            _emit_dummy_out(p, nc, out_d, bar_scratch)
            p.finalize(e(nc.Block()))
            return nc, p

        with ExitStack() as s3:
            e3 = s3.enter_context
            y_acc = e3(nc.sbuf_tensor("y_acc", [128, 8 * D], F32))
            h_sb = e3(nc.sbuf_tensor("h_sb", [128, HC * TB], BF16))
            wd_sb = [e3(nc.sbuf_tensor(f"wd_sb{i}", [128, HC * 1024], BF16)) for i in range(2)]
            wgc = [e3(nc.sbuf_tensor(f"wgc{i}", [128, NDC * 128], BF16)) for i in range(4)]
            wuc = [e3(nc.sbuf_tensor(f"wuc{i}", [128, NDC * 128], BF16)) for i in range(4)]
            sg_half = [e3(nc.sbuf_tensor(f"sgh{i}", [128, 512], BF16)) for i in range(2)]
            ctok = e3(nc.sbuf_tensor("ctok", [128, 64], F32))
            gat = {
                "eu": e3(nc.sbuf_tensor("g_eu", [128, 8], F32)),
                "gsum": e3(nc.sbuf_tensor("g_gsum", [128, 1], F32)),
                "rg": e3(nc.sbuf_tensor("g_rg", [128, 1], F32)),
                "gates": e3(nc.sbuf_tensor("g_gates", [128, 8], F32)),
                "m1": e3(nc.sbuf_tensor("g_m1", [128, 1], F32)),
                "mlt": e3(nc.sbuf_tensor("g_mlt", [128, 8], F32)),
                "g2": e3(nc.sbuf_tensor("g_g2", [128, 8], F32)),
                "m2": e3(nc.sbuf_tensor("g_m2", [128, 1], F32)),
                "keep": e3(nc.sbuf_tensor("g_keep", [128, 8], F32)),
                "den": e3(nc.sbuf_tensor("g_den", [128, 1], F32)),
                "rden": e3(nc.sbuf_tensor("g_rden", [128, 1], F32)),
            }
            st8 = {k: e3(nc.sbuf_tensor(f"s3_{k}", [128, 8], F32))
                   for k in ("sum", "sumsq", "mean", "t1", "m2", "rv", "rstd")}
            y2junk = e3(nc.sbuf_tensor("y2junk", [128, D], F32))

            y1Tb = y1T_blk[:].rearrange("p (dc t) -> p dc t", t=TB)
            yav = y_acc[:].rearrange("p (s d) -> p s d", d=D)
            hv = h_sb[:].rearrange("p (hc t) -> p hc t", t=TB)
            wdv = [wd_sb[i][:].rearrange("p (hc d) -> p hc d", d=1024) for i in range(2)]
            rwv = rw_sb[:].rearrange("p (dc e) -> p dc e", e=8)


            psR = [e3(nc.psum_tensor(f"psR{i}", [128, 8], F32)) for i in range(1)]
            psG = [e3(nc.psum_tensor(f"psG{i}", [128, 512], F32)) for i in range(2)]
            psU = [e3(nc.psum_tensor(f"psU{i}", [128, 512], F32)) for i in range(2)]
            psW = [e3(nc.psum_tensor(f"psW{i}", [128, 512], F32)) for i in range(3)]
            wgc_free = [p.cnt["pe"]] * 4
            wuc_free = [p.cnt["pe"]] * 4
            wd_free = [p.cnt["pe"], p.cnt["pe"]]
            hs_free = [p.cnt["pe"]] * HC
            psR_free = [p.cnt["se"]]
            psG_free = [p.cnt["se"], p.cnt["se"]]
            psU_free = [p.cnt["ve"], p.cnt["ve"]]
            psW_free = [p.cnt["ve"]] * 3
            ctsb_done = 0
            yadd_last = 0
            gu_last_pe = 0
            y1t_at = [0, 0]
            yacc_at = [0, 0]
            NUNIT = NE + 2

            def _unit_srcs(unit):
                if unit < NE:
                    return (wg_p[unit], wu_p[unit],
                            wd_r[unit].rearrange("hc p d -> p hc d"))
                sh = unit - NE
                return (sg_p[sh * HC:(sh + 1) * HC], su_p[sh * HC:(sh + 1) * HC],
                        sd_r[sh * HC:(sh + 1) * HC].rearrange("hc p d -> p hc d"))

            NCHUNK = 2 * NUNIT * HC
            gu_at = {}

            def _issue_gu(gidx):
                if gidx >= NCHUNK:
                    return
                u, hc = divmod(gidx % (NUNIT * HC), HC)
                par = gidx % 4
                wg_src, wu_src, _ = _unit_srcs(u)
                p.wait("gpsimd", "pe", wgc_free[par])
                ga = p.dma("gpsimd",
                           wgc[par][:].rearrange("p (dc j) -> p dc j", j=128),
                           wg_src[hc], f"dg{par % 2}")
                p.wait("gpsimd", "pe", wuc_free[par])
                ua = p.dma("gpsimd",
                           wuc[par][:].rearrange("p (dc j) -> p dc j", j=128),
                           wu_src[hc], f"du{par % 2}")
                gu_at[gidx] = (par, ga, ua)

            def _y1t_load(bb, engs, pe_gate):
                for eng in engs:
                    p.wait(eng, "pe", pe_gate)
                at = 0
                k = 0
                for half in range(2):
                    slot = bb * 2 + half
                    dst = (y1T_blk[:].rearrange("p (dc t) -> p dc t", t=TB)
                           [:, :, half * 512:(half + 1) * 512])
                    src = y1T_park[slot].rearrange("p (dc t) -> p dc t", t=512)
                    for q4 in range(4):
                        at = p.dma(engs[k % len(engs)],
                                   dst[:, q4 * 4:(q4 + 1) * 4, :],
                                   src[:, q4 * 4:(q4 + 1) * 4, :], "lds")
                        k += 1
                y1t_at[bb] = at

            def _yacc_load(bb, engs):
                for eng in engs:
                    p.wait(eng, "ve", p.cnt["ve"])
                    p.wait(eng, "po", p.cnt["po"])
                    p.wait(eng, "do0", p.cnt["do0"])
                    p.wait(eng, "do1", p.cnt["do1"])
                at = 0
                for st in range(4):
                    at = p.dma(engs[st % len(engs)], yav[:, st * 2:(st + 1) * 2, :],
                               y1_park[bb * TB + st * 256:bb * TB + (st + 1) * 256, :]
                               .rearrange("(s p) d -> p s d", p=128), "lds")
                yacc_at[bb] = at

            y1t_at[0] = 0
            _yacc_load(0, ("sync", "scalar"))
            for g0 in range(4):
                _issue_gu(g0)

            ctsb_hold = [0]

            def _emit_router_ts(blkb, ts):
                    rp = 0
                    p.wait("tensor", "se", psR_free[rp])
                    for dc in range(NDC):
                        p.op("tensor",
                             lambda E, o=psR[rp][:], l=y1Tb[:, dc, ts * 128:(ts + 1) * 128],
                             r=rwv[:, dc, :], s=(dc == 0), t=(dc == NDC - 1):
                             E.matmul(o, l, r, start=s, stop=t),
                             inc="pe" if dc == NDC - 1 else None)
                    rmm = p.cnt["pe"]
                    p.wait("scalar", "pe", rmm)
                    p.wait("scalar", "ve", ctsb_hold[0])
                    c = p.op("scalar", lambda E, o=gat["eu"][:], i=psR[rp][:]:
                             E.activation(out=o, in_=i, func=AF.Exp), inc="se")
                    psR_free[rp] = c
                    p.wait("vector", "se", c)
                    p.op("vector", lambda E, o=gat["gsum"][:], i=gat["eu"][:]:
                         E.reduce_sum(o, i, axis=AX.X))
                    p.op("vector", lambda E, o=gat["rg"][:], i=gat["gsum"][:]:
                         E.reciprocal(o, i))
                    p.op("vector", lambda E, o=gat["gates"][:], i=gat["eu"][:],
                         s1=gat["rg"][:]:
                         E.tensor_scalar(o, i, s1, None, ALU.mult))
                    p.op("vector", lambda E, o=gat["m1"][:], i=gat["gates"][:]:
                         E.reduce_max(o, i, axis=AX.X))
                    p.op("vector", lambda E, o=gat["mlt"][:], i=gat["gates"][:],
                         s1=gat["m1"][:]:
                         E.tensor_scalar(o, i, s1, None, ALU.is_lt))
                    p.op("vector", lambda E, o=gat["g2"][:], a=gat["gates"][:],
                         b=gat["mlt"][:]:
                         E.tensor_tensor(o, a, b, ALU.mult))
                    p.op("vector", lambda E, o=gat["m2"][:], i=gat["g2"][:]:
                         E.reduce_max(o, i, axis=AX.X))
                    p.op("vector", lambda E, o=gat["keep"][:], i=gat["gates"][:],
                         s1=gat["m2"][:]:
                         E.tensor_scalar(o, i, s1, None, ALU.is_ge))
                    p.op("vector", lambda E, o=gat["den"][:], i=gat["m1"][:],
                         s1=gat["m2"][:]:
                         E.tensor_scalar(o, i, s1, EPS_GATE, ALU.add, ALU.add))
                    p.op("vector", lambda E, o=gat["rden"][:], i=gat["den"][:]:
                         E.reciprocal(o, i))
                    ctsb_hold[0] = p.op(
                        "vector",
                        lambda E, o=ctok[:, ts * 8:(ts + 1) * 8], i=gat["gates"][:],
                        s1=gat["rden"][:], b=gat["keep"][:]:
                        E.scalar_tensor_tensor(o, i, s1, b, ALU.mult, ALU.mult),
                        inc="ve")


            for blkb in range(2):
                with ExitStack() as s3p:
                    e3p = s3p.enter_context
                    p.wait("tensor", "lds", y1t_at[blkb])
                    p.wait("tensor", "ldg", p.cnt["ldg"])
                    p.wait("vector", "lds", yacc_at[blkb])

                    for unit in range(NUNIT):
                        routed = unit < NE
                        _, _, wd_src = _unit_srcs(unit)
                        wd_at = [0, 0]
                        for dhh in range(2):
                            p.wait("gpsimd", "pe", wd_free[dhh])
                            p.dma("gpsimd", wdv[dhh][:, 0:6, :],
                                  wd_src[:, 0:6, dhh * 1024:(dhh + 1) * 1024],
                                  f"dwd{dhh}")
                            wd_at[dhh] = p.dma("gpsimd", wdv[dhh][:, 6:HC, :],
                                               wd_src[:, 6:HC, dhh * 1024:(dhh + 1) * 1024],
                                               f"dwd{dhh}")
                        for hc in range(HC):
                            gidx = (blkb * NUNIT + unit) * HC + hc
                            par, ga, ua = gu_at.pop(gidx)
                            wgv = wgc[par][:].rearrange("p (dc j) -> p dc j", j=128)
                            wuv = wuc[par][:].rearrange("p (dc j) -> p dc j", j=128)
                            p.wait("tensor", f"dg{par % 2}", ga)
                            p.wait("tensor", f"du{par % 2}", ua)
                            for th in range(2):
                                tpar = th
                                p.wait("tensor", "se", psG_free[tpar])
                                for dc in range(NDC):
                                    p.op("tensor",
                                         lambda E, o=psG[tpar][:], l=wgv[:, dc, :],
                                         r=y1Tb[:, dc, th * 512:(th + 1) * 512],
                                         s=(dc == 0), t=(dc == NDC - 1):
                                         E.matmul(o, l, r, start=s, stop=t),
                                         inc="pe" if dc == NDC - 1 else None)
                                gmm = p.cnt["pe"]
                                p.wait("tensor", "ve", psU_free[tpar])
                                for dc in range(NDC):
                                    p.op("tensor",
                                         lambda E, o=psU[tpar][:], l=wuv[:, dc, :],
                                         r=y1Tb[:, dc, th * 512:(th + 1) * 512],
                                         s=(dc == 0), t=(dc == NDC - 1):
                                         E.matmul(o, l, r, start=s, stop=t),
                                         inc="pe" if dc == NDC - 1 else None)
                                umm = p.cnt["pe"]
                                p.wait("scalar", "pe", gmm)
                                p.wait("scalar", "ve", psU_free[tpar])
                                psG_free[tpar] = p.op(
                                    "scalar",
                                    lambda E, o=sg_half[tpar][:], i=psG[tpar][:]:
                                    E.activation(out=o, in_=i, func=AF.Silu), inc="se")
                                p.wait("vector", "pe", hs_free[hc])
                                p.wait("vector", "se", psG_free[tpar])
                                p.wait("vector", "pe", umm)
                                psU_free[tpar] = p.op(
                                    "vector",
                                    lambda E, o=hv[:, hc, th * 512:(th + 1) * 512],
                                    a=psU[tpar][:], b=sg_half[tpar][:]:
                                    E.tensor_tensor(o, a, b, ALU.mult), inc="ve")
                            wgc_free[par] = p.cnt["pe"]
                            wuc_free[par] = p.cnt["pe"]
                            _issue_gu(gidx + 4)
                            if unit == 0 and hc < 8:
                                _emit_router_ts(blkb, hc)
                        gu_last_pe = p.cnt["pe"]
                        h_done = p.cnt["ve"]
                        p.wait("tensor", "ve", h_done)
                        for dhh in range(2):
                            p.wait("tensor", f"dwd{dhh}", wd_at[dhh])
                            for ts in range(8):
                                for do2 in range(2):
                                    do = dhh * 2 + do2
                                    wpar = (ts * 2 + do2) % 3
                                    p.wait("tensor", "ve", psW_free[wpar])
                                    for hc in range(HC):
                                        p.op("tensor",
                                             lambda E, o=psW[wpar][:],
                                             l=hv[:, hc, ts * 128:(ts + 1) * 128],
                                             r=wdv[dhh][:, hc, do2 * 512:(do2 + 1) * 512],
                                             s=(hc == 0), t=(hc == HC - 1):
                                             E.matmul(o, l, r, start=s, stop=t),
                                             inc="pe" if hc == HC - 1 else None)
                                    wmm = p.cnt["pe"]
                                    p.wait("vector", "pe", wmm)
                                    if routed:
                                        p.wait("vector", "ve", ctsb_hold[0])
                                        psW_free[wpar] = p.op(
                                            "vector",
                                            lambda E, o=yav[:, ts, do * 512:(do + 1) * 512],
                                            i=psW[wpar][:],
                                            sc=ctok[:, ts * 8 + unit:ts * 8 + unit + 1],
                                            b=yav[:, ts, do * 512:(do + 1) * 512]:
                                            E.scalar_tensor_tensor(o, i, sc, b,
                                                                   ALU.mult, ALU.add),
                                            inc="ve")
                                    else:
                                        psW_free[wpar] = p.op(
                                            "vector",
                                            lambda E, o=yav[:, ts, do * 512:(do + 1) * 512],
                                            a=yav[:, ts, do * 512:(do + 1) * 512],
                                            b=psW[wpar][:]:
                                            E.tensor_tensor(o, a, b, ALU.add), inc="ve")
                                    if unit == NUNIT - 1 and dhh == 1 and do2 == 1:
                                        p.op("vector",
                                             lambda E, o=st8["sum"][:, ts:ts + 1],
                                             i=yav[:, ts, :]:
                                             E.reduce_sum(o, i, axis=AX.X))
                                        p.op("vector", lambda E, o=y2junk[:],
                                             i=yav[:, ts, :],
                                             a=st8["sumsq"][:, ts:ts + 1]:
                                             E.scalar_tensor_tensor(
                                                 o, i, 1.0, i, ALU.mult, ALU.mult,
                                                 accum_out=a))
                            wd_free[dhh] = p.cnt["pe"]
                        yadd_last = p.cnt["ve"]
                        for hc in range(HC):
                            hs_free[hc] = p.cnt["pe"]
                    if blkb == 0:
                        _y1t_load(1, ("sync",), gu_last_pe)
                    p.op("vector", lambda E, o=st8["mean"][:], i=st8["sum"][:]:
                         E.tensor_scalar(o, i, 1.0 / D, None, ALU.mult))
                    p.op("vector", lambda E, o=st8["t1"][:], i=st8["sumsq"][:]:
                         E.tensor_scalar(o, i, 1.0 / D, None, ALU.mult))
                    p.op("vector", lambda E, o=st8["m2"][:], a=st8["mean"][:]:
                         E.tensor_tensor(o, a, a, ALU.mult))
                    p.op("vector", lambda E, o=st8["t1"][:], a=st8["t1"][:],
                         b=st8["m2"][:]: E.tensor_tensor(o, a, b, ALU.subtract))
                    p.op("vector", lambda E, o=st8["t1"][:], i=st8["t1"][:]:
                         E.tensor_scalar(o, i, EPS_LN, None, ALU.add))
                    cstat = p.op("vector", lambda E, o=st8["rv"][:], i=st8["t1"][:]:
                                 E.reciprocal(o, i), inc="ve")
                    p.wait("scalar", "ve", cstat)
                    csq = p.op("scalar", lambda E, o=st8["rstd"][:], i=st8["rv"][:]:
                               E.activation(out=o, in_=i, func=AF.Sqrt), inc="se")
                    p.wait("gpsimd", "se", csq)
                    for ts in range(8):
                        par = ts % 2
                        c = p.op("gpsimd",
                                 lambda E, o=yav[:, ts, :], i=yav[:, ts, :],
                                 m=st8["mean"][:, ts:ts + 1], r=st8["rstd"][:, ts:ts + 1]:
                                 E.tensor_scalar(o, i, m, r, ALU.subtract, ALU.mult),
                                 inc="po")
                        p.wait("sync", "po", c)
                        p.dma(
                            "sync",
                            out_d[(blkb * 8 + ts) * 128:(blkb * 8 + ts + 1) * 128, :],
                            yav[:, ts, :], f"do{par}")
                    if blkb == 0:
                        _yacc_load(1, ("sync",))
            p.barrier(bar_scratch)

        p.finalize(e(nc.Block()))
        return nc, p


def _emit_dummy_out(p, nc, out_d, bar_scratch):
    z = nc.sbuf_tensor("zline", [128, D], F32)
    with z as zt:
        p.op("vector", lambda E: E.memset(zt[:], 0.0), inc="ve")
        p.wait("sync", "ve", p.cnt["ve"])
        for i in range(NSLOT * 512 // 128):
            p.dma("sync", out_d[i * 128:(i + 1) * 128, :], zt[:], "dpark")
        p.wait("sync", "dpark", p.cnt["dpark"])
        p.barrier(bar_scratch)



def _host_tables():
    half = DH // 2
    theta = 1.0 / ROPE_BASE ** (np.arange(half, dtype=np.float64) / half)
    pos = np.arange(T, dtype=np.float64)
    fr = np.outer(theta, pos)
    csf = np.concatenate([np.cos(fr), np.sin(fr)], axis=0).astype(np.float32)
    ident = np.eye(128, dtype=np.float32)
    return csf, ident


def _etab_for_role(r):
    p = np.arange(128, dtype=np.int64)[:, None]
    y = np.arange(YT, dtype=np.int64)[None, :]
    v = (p - y + (CPRIME - r * 512)).astype(np.float64)
    slopes = 2.0 ** (-8.0 * np.arange(1, H + 1, dtype=np.float64) / H)
    out = np.zeros((H, 128, YT), dtype=ml_dtypes.bfloat16)
    with np.errstate(under="ignore"):
        for h in range(H):
            e = np.where(v <= 0, np.exp(slopes[h] * np.minimum(v, 0.0)), 0.0)
            out[h] = e.astype(ml_dtypes.bfloat16)
    return out


def kernel(L0_attn_x, L0_attn_Wq, L0_attn_Wk, L0_attn_Wv, L0_attn_Wo,
           L0_ffn_router_w, L0_ffn_wg, L0_ffn_wu, L0_ffn_wd,
           L0_ffn_sg, L0_ffn_su, L0_ffn_sd):
    x = np.ascontiguousarray(L0_attn_x, dtype=np.float32)

    wq_p = np.ascontiguousarray(
        L0_attn_Wq.reshape(NDC, 128, 16, 128).transpose(1, 0, 2, 3))
    wk_p = np.ascontiguousarray(L0_attn_Wk.reshape(NDC, 128, 512).transpose(1, 0, 2))
    wv_p = np.ascontiguousarray(L0_attn_Wv.reshape(NDC, 128, 512).transpose(1, 0, 2))
    wo_p = np.ascontiguousarray(L0_attn_Wo.reshape(NDC, 128, D).transpose(1, 0, 2))
    rw_p = np.ascontiguousarray(L0_ffn_router_w.T.reshape(NDC, 128, 8).transpose(1, 0, 2))
    wg_p = np.ascontiguousarray(
        L0_ffn_wg.reshape(NE, NDC, 128, HC, 128).transpose(0, 3, 2, 1, 4))
    wu_p = np.ascontiguousarray(
        L0_ffn_wu.reshape(NE, NDC, 128, HC, 128).transpose(0, 3, 2, 1, 4))
    wd_r = np.ascontiguousarray(L0_ffn_wd.reshape(NE, HC, 128, D))
    sg_p = np.ascontiguousarray(
        L0_ffn_sg.reshape(NDC, 128, 2 * HC, 128).transpose(2, 1, 0, 3))
    su_p = np.ascontiguousarray(
        L0_ffn_su.reshape(NDC, 128, 2 * HC, 128).transpose(2, 1, 0, 3))
    sd_r = np.ascontiguousarray(L0_ffn_sd.reshape(2 * HC, 128, D))

    csf, ident = _host_tables()
    etab_role = [_etab_for_role(0), _etab_for_role(1)]

    nc, _ = build_program()

    in_maps = []
    own_rows = {}
    for c in range(8):
        b, r = c // 2, c % 2
        slots = [2 * i + r for i in range(NSLOT)]
        own_rows[c] = slots
        rows = np.concatenate([np.arange(s * 512, (s + 1) * 512) for s in slots])
        xo = np.ascontiguousarray(x[b][rows])
        cso = np.ascontiguousarray(csf[:, rows])
        in_maps.append({
            "xf": x[b], "xo": xo,
            "wq_p": wq_p, "wk_p": wk_p, "wv_p": wv_p, "wo_p": wo_p, "rw_p": rw_p,
            "wg_p": wg_p, "wu_p": wu_p, "wd_r": wd_r,
            "sg_p": sg_p, "su_p": su_p, "sd_r": sd_r,
            "csf": csf, "cso": cso, "etab": etab_role[r],
            "ident": ident,
        })

    res = run_bass_kernel_spmd(nc, in_maps, core_ids=list(range(8)))

    out = np.empty((B, T, D), dtype=np.float32)
    for c in range(8):
        b = c // 2
        oc = np.asarray(res.results[c]["out"])
        for i, s in enumerate(own_rows[c]):
            out[b, s * 512:(s + 1) * 512, :] = oc[i * 512:(i + 1) * 512, :]
    return out




# version 18
# speedup vs baseline: 1.0627x; 1.0627x over previous
import numpy as np
import ml_dtypes
from contextlib import ExitStack

import concourse.bass as bass
import concourse.mybir as mybir
from concourse.bass_utils import run_bass_kernel_spmd

F32 = mybir.dt.float32
BF16 = mybir.dt.bfloat16
AF = mybir.ActivationFunctionType
ALU = mybir.AluOpType
AX = mybir.AxisListType

B, T, D = 4, 4096, 2048
H, G, DH = 16, 4, 128
NE, HF, HS = 8, 1408, 2816
NDC = D // 128
HC = HF // 128
ROPE_BASE = 10000.0
SCALE = 1.0 / np.sqrt(DH)
EXP_BIAS = -8.0
EPS_LN = 1e-5
EPS_GATE = 1e-20
NSLOT = 4
YT = 4480
CPRIME = 896
MASK_NEG = -60000.0

DBG_TAPS = set()
P0_BLOCKS = 8

ENGINES = ("tensor", "vector", "scalar", "sync", "gpsimd")


class Prog:

    def __init__(self, nc):
        self.nc = nc
        self.q = {e: [] for e in ENGINES}
        self.sems = {}
        self.cnt = {}
        self.last_wait = {e: {} for e in ENGINES}
        self.bar_k = 0
        self.taps = {}

    def mksem(self, stack, *names):
        for n in names:
            self.sems[n] = stack.enter_context(self.nc.semaphore(n))
            self.cnt[n] = 0

    def mkdmasem(self, stack, eng, *names):
        if not hasattr(self, "dma_sems"):
            self.dma_sems = []
        for n in names:
            self.sems[n] = stack.enter_context(self.nc.semaphore(n))
            self.cnt[n] = 0
            self.dma_sems.append((eng, n))

    AUTO_SEM = {"vector": "ve", "scalar": "se"}

    def op(self, eng, build, inc=None, amt=1):
        auto = self.AUTO_SEM.get(eng)
        if auto is not None:
            self.wait(eng, auto, self.cnt[auto])
            assert inc is None or inc == auto
            inc = auto
        semh = self.sems[inc] if inc else None
        if inc:
            self.cnt[inc] += amt

        def go(E, build=build, semh=semh, amt=amt):
            ins = build(E)
            if semh is not None:
                ins.then_inc(semh, amt)
        self.q[eng].append(go)
        return self.cnt[inc] if inc else None

    def op_relaxed(self, eng, build, amt=1):
        auto = self.AUTO_SEM[eng]
        self.cnt[auto] += amt
        semh = self.sems[auto]

        def go(E, build=build, semh=semh, amt=amt):
            build(E).then_inc(semh, amt)
        self.q[eng].append(go)
        return self.cnt[auto]

    def wait(self, eng, sem, val=None):
        if val is None:
            val = self.cnt[sem]
        if val <= 0:
            return
        lw = self.last_wait[eng]
        if lw.get(sem, -1) >= val:
            return
        lw[sem] = val
        semh = self.sems[sem]
        self.q[eng].append(lambda E, s=semh, v=val: E.wait_ge(s, v))

    def dma(self, eng, out, in_, sem):
        semh = self.sems[sem]
        self.cnt[sem] += 16

        def go(E, out=out, in_=in_, semh=semh):
            E.dma_start(out=out, in_=in_).then_inc(semh, 16)
        self.q[eng].append(go)
        return self.cnt[sem]

    def barrier(self, bar_scratch=None):
        for e in ENGINES:
            for n in ("pe", "ve", "se", "po"):
                self.wait(e, n)
            for _eng, n in getattr(self, "dma_sems", []):
                self.wait(e, n)

    def tap(self, name, ap_fn, waits):
        if name not in DBG_TAPS:
            return
        shape = ap_fn("shape")
        dt = ap_fn("dtype")
        t = self.nc.dram_tensor("tap_" + name, shape, dt, kind="ExternalOutput").ap()
        self.taps[name] = True
        for sem, val in waits:
            self.wait("sync", sem, val)
        self.dma("sync", t, ap_fn("ap"), "dt")

    def finalize(self, block):
        block.gpsimd(lambda E: [f(E) for f in self.q["gpsimd"]])
        block.sync(lambda E: [f(E) for f in self.q["sync"]])
        block.tensor(lambda E: [f(E) for f in self.q["tensor"]])
        block.vector(lambda E: [f(E) for f in self.q["vector"]])
        block.scalar(lambda E: [f(E) for f in self.q["scalar"]])


def _mk_tap(tile, sl=None):
    def fn(what):
        ap = tile[:] if sl is None else sl
        if what == "shape":
            return list(ap.shape)
        if what == "dtype":
            return ap.dtype
        return ap
    return fn



def emit_transpose_tile(p, psT, ident_bf, src_tile, dst_view, tt, pe_state):
    for g4 in range(4):
        par = (tt * 4 + g4) % 2
        p.wait("tensor", "se", pe_state["psT_free"][par])
        for j in range(4):
            dc = g4 * 4 + j
            p.op("tensor",
                 lambda E, o=psT[par][:, j * 128:(j + 1) * 128],
                 i=src_tile[:, dc * 128:(dc + 1) * 128], idn=ident_bf[:]:
                 E.transpose(o, i, idn),
                 inc="pe" if j == 3 else None)
        done = p.cnt["pe"]
        p.wait("scalar", "pe", done)
        out = dst_view[:, g4 * 4:(g4 + 1) * 4, tt * 128:(tt + 1) * 128]
        src = psT[par][:].rearrange("p (j t) -> p j t", t=128)
        c = p.op("scalar", lambda E, o=out, i=src: E.copy(o, i), inc="se")
        pe_state["psT_free"][par] = c


def emit_rope(p, psQK, cs, cs_off, dst_lo, dst_hi, scr1, scr2, pe_count):
    p.wait("vector", "pe", pe_count)
    cosap = cs[0:64, cs_off:cs_off + 512]
    sinap = cs[64:128, cs_off:cs_off + 512]
    p.op("vector", lambda E, o=scr1[0:64, :], a=psQK[0:64, :], b=cosap:
         E.tensor_tensor(o, a, b, ALU.mult))
    p.op("vector", lambda E, o=scr2[0:64, :], a=psQK[64:128, :], b=sinap:
         E.tensor_tensor(o, a, b, ALU.mult))
    p.op("vector", lambda E, o=dst_lo, a=scr1[0:64, :], b=scr2[0:64, :]:
         E.tensor_tensor(o, a, b, ALU.subtract), inc="ve")
    p.op("vector", lambda E, o=scr1[0:64, :], a=psQK[0:64, :], b=sinap:
         E.tensor_tensor(o, a, b, ALU.mult))
    p.op("vector", lambda E, o=scr2[0:64, :], a=psQK[64:128, :], b=cosap:
         E.tensor_tensor(o, a, b, ALU.mult))
    return p.op("vector", lambda E, o=dst_hi, a=scr1[0:64, :], b=scr2[0:64, :]:
                E.tensor_tensor(o, a, b, ALU.add), inc="ve")


def emit_ln_stats_normalize(p, y_in, y_out, st, inv_n, junk):
    n = y_in.shape[-1]
    p.op("vector", lambda E, o=st["sum"][:], i=y_in: E.reduce_sum(o, i, axis=AX.X))
    p.op("vector", lambda E, o=junk, i=y_in, a=st["sumsq"][:]:
         E.scalar_tensor_tensor(o, i, 1.0, i, ALU.mult, ALU.mult, accum_out=a))
    p.op("vector", lambda E, o=st["mean"][:], i=st["sum"][:]:
         E.tensor_scalar(o, i, inv_n, None, ALU.mult))
    p.op("vector", lambda E, o=st["t1"][:], i=st["sumsq"][:]:
         E.tensor_scalar(o, i, inv_n, None, ALU.mult))
    p.op("vector", lambda E, o=st["m2"][:], a=st["mean"][:]:
         E.tensor_tensor(o, a, a, ALU.mult))
    p.op("vector", lambda E, o=st["t1"][:], a=st["t1"][:], b=st["m2"][:]:
         E.tensor_tensor(o, a, b, ALU.subtract))
    p.op("vector", lambda E, o=st["t1"][:], i=st["t1"][:]:
         E.tensor_scalar(o, i, EPS_LN, None, ALU.add))
    c = p.op("vector", lambda E, o=st["rv"][:], i=st["t1"][:]:
             E.reciprocal(o, i), inc="ve")
    p.wait("scalar", "ve", c)
    cs = p.op("scalar", lambda E, o=st["rstd"][:], i=st["rv"][:]:
              E.activation(out=o, in_=i, func=AF.Sqrt), inc="se")
    p.wait("vector", "se", cs)
    return p.op("vector", lambda E, o=y_out, i=y_in, m=st["mean"][:], r=st["rstd"][:]:
                E.tensor_scalar(o, i, m, r, ALU.subtract, ALU.mult), inc="ve")



def build_program(phase_limit=4):
    nc = bass.Bass("TRN2", target_bir_lowering=False)

    xf = nc.dram_tensor("xf", [T, D], F32, kind="ExternalInput").ap()
    xo = nc.dram_tensor("xo", [NSLOT * 512, D], F32, kind="ExternalInput").ap()
    etab = nc.dram_tensor("etab", [H, 128, YT], BF16, kind="ExternalInput").ap()
    wq_p = nc.dram_tensor("wq_p", [128, NDC, 16, 128], F32, kind="ExternalInput").ap()
    wk_p = nc.dram_tensor("wk_p", [128, NDC, 512], F32, kind="ExternalInput").ap()
    wv_p = nc.dram_tensor("wv_p", [128, NDC, 512], F32, kind="ExternalInput").ap()
    wo_p = nc.dram_tensor("wo_p", [128, NDC, D], F32, kind="ExternalInput").ap()
    rw_p = nc.dram_tensor("rw_p", [128, NDC, 8], F32, kind="ExternalInput").ap()
    wg_p = nc.dram_tensor("wg_p", [NE, HC, 128, NDC, 128], F32, kind="ExternalInput").ap()
    wu_p = nc.dram_tensor("wu_p", [NE, HC, 128, NDC, 128], F32, kind="ExternalInput").ap()
    wd_r = nc.dram_tensor("wd_r", [NE, HC, 128, D], F32, kind="ExternalInput").ap()
    sg_p = nc.dram_tensor("sg_p", [2 * HC, 128, NDC, 128], F32, kind="ExternalInput").ap()
    su_p = nc.dram_tensor("su_p", [2 * HC, 128, NDC, 128], F32, kind="ExternalInput").ap()
    sd_r = nc.dram_tensor("sd_r", [2 * HC, 128, D], F32, kind="ExternalInput").ap()
    csf = nc.dram_tensor("csf", [128, T], F32, kind="ExternalInput").ap()
    cso_in = nc.dram_tensor("cso", [128, NSLOT * 512], F32, kind="ExternalInput").ap()
    ident_in = nc.dram_tensor("ident", [128, 128], F32, kind="ExternalInput").ap()
    out_d = nc.dram_tensor("out", [NSLOT * 512, D], F32, kind="ExternalOutput").ap()

    attnT_park = nc.dram_tensor("attnT_park", [NSLOT, 128, H * 512], BF16, kind="Internal").ap()
    y1_park = nc.dram_tensor("y1_park", [NSLOT * 512, D], F32, kind="Internal").ap()
    y1T_park = nc.dram_tensor("y1T_park", [NSLOT, 128, NDC * 512], BF16, kind="Internal").ap()
    wq_bf = nc.dram_tensor("wq_bf", [128, NDC, 16, 128], BF16, kind="Internal").ap()
    wo_bf = nc.dram_tensor("wo_bf", [128, NDC, D], BF16, kind="Internal").ap()

    with ExitStack() as g:
        e = g.enter_context
        p = Prog(nc)
        p.mksem(g, "pe", "ve", "se", "po")
        p.mkdmasem(g, "gpsimd", "ldg", "dkv", "dxs", "dx0", "dx1", "dq0", "dq1",
                   "dg0", "dg1", "du0", "du1", "dwd0", "dwd1")
        p.mkdmasem(g, "sync", "lds", "det", "dxr0", "dxr1", "dat", "dpark", "dy1t", "dy0", "dy1", "do0", "do1", "dt")
        p.mkdmasem(g, "scalar", "dwi", "dwo")
        p.mkdmasem(g, "gpsimd", "dtg")

        ident_bf = e(nc.sbuf_tensor("ident_bf", [128, 128], BF16))
        ones_sq = e(nc.sbuf_tensor("ones_sq", [128, 128], BF16))
        expbias = e(nc.sbuf_tensor("expbias", [128, 1], F32))
        rw_sb = e(nc.sbuf_tensor("rw_sb", [128, NDC * 8], BF16))
        bar_scratch = None
        zeros_c = e(nc.sbuf_tensor("zeros_c", [128, 1], F32))
        ones_c = e(nc.sbuf_tensor("ones_c", [128, 1], F32))
        nc.const_aps.register(F32, 0.0, zeros_c[:])
        nc.const_aps.register(F32, 1.0, ones_c[:])
        p.op("vector", lambda E: E.memset(zeros_c[:], 0.0))
        p.op("vector", lambda E: E.memset(ones_c[:], 1.0))

        p.dma("gpsimd", ident_bf[:], ident_in, "ldg")
        p.dma("gpsimd", rw_sb[:], rw_p.rearrange("p dc e -> p (dc e)"), "ldg")
        p.op("vector", lambda E: E.memset(ones_sq[:], 1.0))
        p.op("vector", lambda E: E.memset(expbias[:], EXP_BIAS), inc="ve")
        consts_ve = p.cnt["ve"]
        p.wait("scalar", "ve", consts_ve)

        with ExitStack() as kv_era:
            ek = kv_era.enter_context
            K_sb = ek(nc.sbuf_tensor("K_sb", [128, G * T], BF16))
            V_sb = ek(nc.sbuf_tensor("V_sb", [128, (T // 128) * 512], BF16))
            Kv = K_sb[:].rearrange("p (g t) -> p g t", g=G)
            Vv = V_sb[:].rearrange("p (kc t) -> p kc t", t=512)

            with ExitStack() as s0:
                e0 = s0.enter_context
                csf_sb = e0(nc.sbuf_tensor("csf_sb", [128, T], F32))
                p.dma("sync", csf_sb[:], csf, "lds")
                wk_sb = e0(nc.sbuf_tensor("wk_sb", [128, NDC * 512], BF16))
                wv_sb = e0(nc.sbuf_tensor("wv_sb", [128, NDC * 512], BF16))
                x_st = [e0(nc.sbuf_tensor(f"x_st{i}", [128, 4 * D], BF16)) for i in range(2)]
                xT_blk = e0(nc.sbuf_tensor("xT_blk", [128, NDC * 512], BF16))
                rsc1 = e0(nc.sbuf_tensor("rsc1", [64, 512], F32))
                rsc2 = e0(nc.sbuf_tensor("rsc2", [64, 512], F32))
                psT = [e0(nc.psum_tensor(f"psT{i}", [128, 512], BF16)) for i in range(2)]
                psK = [e0(nc.psum_tensor(f"psK{i}", [128, 512], F32)) for i in range(4)]
                psV = [e0(nc.psum_tensor(f"psV{i}", [128, 512], F32)) for i in range(2)]

                p.dma("gpsimd", wk_sb[:].rearrange("p (dc c) -> p dc c", c=512), wk_p, "dkv")
                p.dma("gpsimd", wv_sb[:].rearrange("p (dc c) -> p dc c", c=512), wv_p, "dkv")
                w_dkv = p.cnt["dkv"]

                wqc_st = [e0(nc.sbuf_tensor("wqc_st0", [128, 8192], BF16))]
                wcast_out_at = [0]

                def _emit_cast(qq):
                    cpar = 0
                    if qq < 4:
                        src = wq_p[:, :, qq * 4:(qq + 1) * 4, :]
                        dstv = wqc_st[cpar][:].rearrange("p (dc h j) -> p dc h j", h=4, j=128)
                        dst = wq_bf[:, :, qq * 4:(qq + 1) * 4, :]
                    else:
                        q4 = qq - 4
                        src = wo_p[:, :, q4 * 512:(q4 + 1) * 512]
                        dstv = wqc_st[cpar][:].rearrange("p (dc c) -> p dc c", c=512)
                        dst = wo_bf[:, :, q4 * 512:(q4 + 1) * 512]
                    p.wait("gpsimd", "dwo", wcast_out_at[cpar])
                    at = p.dma("gpsimd", dstv, src, "dwi")
                    p.wait("scalar", "dwi", at)
                    wcast_out_at[cpar] = p.dma("scalar", dst, dstv, "dwo")
                wkv = wk_sb[:].rearrange("p (dc c) -> p dc c", c=512)
                wvv = wv_sb[:].rearrange("p (dc c) -> p dc c", c=512)
                xTv = xT_blk[:].rearrange("p (dc t) -> p dc t", t=512)

                p.wait("vector", "lds", p.cnt["lds"])
                p.wait("tensor", "ldg", p.cnt["ldg"])
                pe_state = {"psT_free": [0, 0]}
                x_free = [0, 0]
                xT_free = 0
                psK_free = [0, 0, 0, 0]
                psV_free = [0, 0]
                xs_at = [0, 0]

                def _p0_stage(bb):
                    par = bb % 2
                    p.wait("gpsimd", "pe", x_free[par])
                    at = 0
                    for tt in range(4):
                        tg = bb * 4 + tt
                        at = p.dma("gpsimd", x_st[par][:, tt * D:(tt + 1) * D],
                                   xf[tg * 128:(tg + 1) * 128, :], f"dx{par}")
                    xs_at[par] = at

                if P0_BLOCKS:
                    _p0_stage(0)
                for blk in range(P0_BLOCKS):
                    par = blk % 2
                    if blk + 1 < P0_BLOCKS:
                        _p0_stage(blk + 1)
                    _emit_cast(blk)
                    p.wait("tensor", f"dx{par}", xs_at[par])
                    p.wait("tensor", "pe", xT_free)
                    for tt in range(4):
                        emit_transpose_tile(p, psT, ident_bf,
                                            x_st[par][:, tt * D:(tt + 1) * D],
                                            xTv, tt, pe_state)
                    x_free[par] = p.cnt["pe"]
                    p.wait("tensor", "se", p.cnt["se"])
                    p.wait("tensor", "dkv", w_dkv)
                    for gi in range(G):
                        kpar = gi % 4
                        p.wait("tensor", "ve", psK_free[kpar])
                        for dc in range(NDC):
                            p.op("tensor",
                                 lambda E, o=psK[kpar][:], l=wkv[:, dc, gi * 128:(gi + 1) * 128],
                                 r=xTv[:, dc, :], s=(dc == 0), t=(dc == NDC - 1):
                                 E.matmul(o, l, r, start=s, stop=t),
                                 inc="pe" if dc == NDC - 1 else None)
                        psK_free[kpar] = emit_rope(
                            p, psK[kpar], csf_sb[:], blk * 512,
                            Kv[0:64, gi, blk * 512:(blk + 1) * 512],
                            Kv[64:128, gi, blk * 512:(blk + 1) * 512],
                            rsc1, rsc2, p.cnt["pe"])
                    for ts4 in range(4):
                        kc = blk * 4 + ts4
                        vpar = ts4 % 2
                        p.wait("tensor", "ve", psV_free[vpar])
                        for dc in range(NDC):
                            p.op("tensor",
                                 lambda E, o=psV[vpar][:], l=xTv[:, dc, ts4 * 128:(ts4 + 1) * 128],
                                 r=wvv[:, dc, :], s=(dc == 0), t=(dc == NDC - 1):
                                 E.matmul(o, l, r, start=s, stop=t),
                                 inc="pe" if dc == NDC - 1 else None)
                        p.wait("vector", "pe", p.cnt["pe"])
                        psV_free[vpar] = p.op("vector",
                                              lambda E, o=Vv[:, kc, :], i=psV[vpar][:]:
                                              E.tensor_copy(o, i), inc="ve")
                    xT_free = p.cnt["pe"]

                p.tap("K_sb", _mk_tap(K_sb), [("ve", p.cnt["ve"])])
                p.tap("V_sb", _mk_tap(V_sb), [("ve", p.cnt["ve"])])
                p.barrier(bar_scratch)

            if phase_limit < 1:
                _emit_dummy_out(p, nc, out_d, bar_scratch)
                p.finalize(e(nc.Block()))
                return nc, p

            E_h = [ek(nc.sbuf_tensor(f"E_h{i}", [128, YT], BF16)) for i in range(4)]
            eh_war = [0, 0, 0, 0]
            eh_at = [0, 0, 0, 0]
            cso_sb = ek(nc.sbuf_tensor("cso_sb", [128, NSLOT * 512], F32))
            p.dma("sync", cso_sb[:], cso_in, "lds")
            p1_lds = p.cnt["lds"]

            x_stage = ek(nc.sbuf_tensor("x_stage", [128, 4 * D], BF16))
            xstage_free = 0

            def _prefetch_xstage(sl):
                p.wait("gpsimd", "pe", xstage_free)
                at = 0
                for tt in range(4):
                    at = p.dma("gpsimd", x_stage[:, tt * D:(tt + 1) * D],
                               xo[(sl * 4 + tt) * 128:(sl * 4 + tt + 1) * 128, :],
                               "dxs")
                return at

            xs_at = _prefetch_xstage(0)

            for slot in range(NSLOT):
                with ExitStack() as s1:
                    e1 = s1.enter_context
                    Q_all = e1(nc.sbuf_tensor(f"Q_all{slot}", [128, H * 512], BF16))
                    attnT = e1(nc.sbuf_tensor(f"attnT{slot}", [128, H * 512], BF16))
                    with ExitStack() as s1b:
                        e1b = s1b.enter_context
                        xT_e = e1b(nc.sbuf_tensor(f"xT_e{slot}", [128, NDC * 512], BF16))
                        wq_sb = [e1b(nc.sbuf_tensor(f"wq_sb{slot}_{i}", [128, NDC * 512], BF16)) for i in range(2)]
                        rsc1 = e1b(nc.sbuf_tensor(f"rsc1b{slot}", [64, 512], F32))
                        rsc2 = e1b(nc.sbuf_tensor(f"rsc2b{slot}", [64, 512], F32))
                        psT = [e1b(nc.psum_tensor(f"psT1{slot}_{i}", [128, 512], BF16)) for i in range(2)]
                        psQ = [e1b(nc.psum_tensor(f"psQ{slot}_{i}", [128, 512], F32)) for i in range(4)]

                        xTv = xT_e[:].rearrange("p (dc t) -> p dc t", t=512)
                        pe_state = {"psT_free": [0, 0]}
                        p.wait("tensor", "dxs", xs_at)
                        for tt in range(4):
                            emit_transpose_tile(p, psT, ident_bf,
                                                x_stage[:, tt * D:(tt + 1) * D],
                                                xTv, tt, pe_state)
                        xstage_free = p.cnt["pe"]
                        p.wait("tensor", "se", p.cnt["se"])
                        if slot + 1 < NSLOT:
                            xs_at = _prefetch_xstage(slot + 1)
                        p.wait("vector", "lds", p1_lds)

                        wq_free = [0, 0]
                        wq_at = [0, 0]
                        psQ_free = [0, 0, 0, 0]

                        def _wq_dma(qtr):
                            par = qtr % 2
                            dstv = wq_sb[par][:].rearrange("p (dc h j) -> p dc h j",
                                                           h=4, j=128)
                            p.wait("gpsimd", "pe", wq_free[par])
                            p.wait("scalar", "pe", wq_free[par])
                            p.dma("gpsimd", dstv[:, 0:8, :, :],
                                  wq_bf[:, 0:8, qtr * 4:(qtr + 1) * 4, :], f"dq{par}")
                            wq_at[par] = p.dma(
                                "scalar", dstv[:, 8:16, :, :],
                                wq_bf[:, 8:16, qtr * 4:(qtr + 1) * 4, :], f"dq{par}")
                        _wq_dma(0)
                        _wq_dma(1)
                        for qtr in range(4):
                            par = qtr % 2
                            p.wait("tensor", f"dq{par}", wq_at[par])
                            wqv = wq_sb[par][:].rearrange("p (dc c) -> p dc c", c=512)
                            for hh in range(4):
                                h = qtr * 4 + hh
                                qpar = h % 4
                                p.wait("tensor", "ve", psQ_free[qpar])
                                for dc in range(NDC):
                                    p.op("tensor",
                                         lambda E, o=psQ[qpar][:], l=wqv[:, dc, hh * 128:(hh + 1) * 128],
                                         r=xTv[:, dc, :], s=(dc == 0), t=(dc == NDC - 1):
                                         E.matmul(o, l, r, start=s, stop=t),
                                         inc="pe" if dc == NDC - 1 else None)
                                psQ_free[qpar] = emit_rope(
                                    p, psQ[qpar], cso_sb[:], slot * 512,
                                    Q_all[0:64, h * 512:(h + 1) * 512],
                                    Q_all[64:128, h * 512:(h + 1) * 512],
                                    rsc1, rsc2, p.cnt["pe"])
                            if qtr < 2:
                                wq_free[par] = p.cnt["pe"]
                                _wq_dma(qtr + 2)
                        p.wait("tensor", "ve", p.cnt["ve"])
                        p.tap(f"Q_{slot}", _mk_tap(Q_all), [("ve", p.cnt["ve"])])

                    NB = 4
                    with ExitStack() as s1c:
                        e1c = s1c.enter_context
                        ew = 1408 + 1024 * slot
                        expt = [e1c(nc.sbuf_tensor(f"expt{slot}_{i}", [128, 512], BF16)) for i in range(NB)]
                        expw = [e1c(nc.sbuf_tensor(f"expw{slot}_{i}", [128, 512], BF16)) for i in range(NB)]
                        rb_sb = e1c(nc.sbuf_tensor(f"rb_sb{slot}", [128, 512], F32))
                        psS = [e1c(nc.psum_tensor(f"psS{slot}_{i}", [128, 512], F32)) for i in range(NB)]
                        psSum = [e1c(nc.psum_tensor(f"psSum{slot}_{i}", [128, 512], F32)) for i in range(2)]
                        psAV = [e1c(nc.psum_tensor(f"psAV{slot}_{i}", [128, 512], F32)) for i in range(2)]

                        nk = 8 * slot + 8
                        LAG = 3
                        norm_done = [p.cnt["ve"], p.cnt["ve"]]
                        pending = [None]

                        def _eh_load(h):
                            epar = h % 4
                            half = (ew // 1024) * 512 + 512
                            p.wait("sync", "ve", eh_war[epar])
                            p.wait("gpsimd", "ve", eh_war[epar])
                            a1 = p.dma("sync", E_h[epar][:, 0:half],
                                       etab[h][:, 0:half], "det")
                            a2 = p.dma("gpsimd", E_h[epar][:, half:ew],
                                       etab[h][:, half:ew], "dtg")
                            eh_at[epar] = (a1, a2)
                        for hh0 in range(4):
                            _eh_load(hh0)

                        def _normalize(h, mm_last):
                            hp = h % 2
                            p.wait("vector", "pe", mm_last)
                            p.op("vector", lambda E, o=rb_sb[:], i=psSum[hp][:]:
                                 E.reciprocal(o, i))
                            nd = p.op(
                                "vector",
                                lambda E, o=attnT[:, h * 512:(h + 1) * 512],
                                a=psAV[hp][:], b=rb_sb[:]:
                                E.tensor_tensor(o, a, b, ALU.mult), inc="ve")
                            norm_done[hp] = nd
                            return nd

                        for h in range(H):
                            gi = h // 4
                            epar = h % 4
                            hp = h % 2
                            p.wait("vector", "det", eh_at[epar][0])
                            p.wait("vector", "dtg", eh_at[epar][1])
                            exp_done = [p.cnt["se"]] * NB
                            tt_done = [p.cnt["ve"]] * NB
                            mm_done = [p.cnt["pe"]] * NB
                            sc_done = [0] * NB
                            for step in range(nk + LAG):
                                kc = step
                                if kc < nk:
                                    bpar = kc % NB
                                    p.wait("tensor", "se", exp_done[bpar])
                                    sc_done[bpar] = p.op(
                                        "tensor",
                                        lambda E, o=psS[bpar][:],
                                        l=Kv[:, gi, kc * 128:(kc + 1) * 128],
                                        r=Q_all[:, h * 512:(h + 1) * 512]:
                                        E.matmul(o, l, r, start=True, stop=True),
                                        inc="pe")
                                if kc == LAG - 1 and pending[0] is not None:
                                    norm_prev = pending[0]
                                    pending[0] = None
                                    norm_prev()
                                j = step - LAG
                                if j < 0:
                                    continue
                                bpar = j % NB
                                p.wait("scalar", "pe", sc_done[bpar])
                                p.wait("scalar", "ve", tt_done[bpar])
                                exp_done[bpar] = p.op_relaxed(
                                    "scalar",
                                    lambda E, o=expt[bpar][:], i=psS[bpar][:]:
                                    E.activation(out=o, in_=i, func=AF.Exp,
                                                 bias=expbias[:], scale=SCALE))
                                y0 = CPRIME + slot * 1024 - j * 128
                                p.wait("vector", "se", exp_done[bpar])
                                p.wait("vector", "pe", mm_done[bpar])
                                tt_done[bpar] = p.op_relaxed(
                                    "vector",
                                    lambda E, o=expw[bpar][:], a=expt[bpar][:],
                                    b=E_h[epar][:, y0:y0 + 512]:
                                    E.tensor_tensor(o, a, b, ALU.mult))
                                p.wait("tensor", "ve", tt_done[bpar])
                                if j == 0:
                                    p.wait("tensor", "ve", norm_done[hp])
                                p.op("tensor",
                                     lambda E, o=psSum[hp][:], l=ones_sq[:], r=expw[bpar][:],
                                     s=(j == 0), t=(j == nk - 1):
                                     E.matmul(o, l, r, start=s, stop=t))
                                mm_done[bpar] = p.op(
                                    "tensor",
                                    lambda E, o=psAV[hp][:], l=Vv[:, j, gi * 128:(gi + 1) * 128],
                                    r=expw[bpar][:], s=(j == 0), t=(j == nk - 1):
                                    E.matmul(o, l, r, start=s, stop=t),
                                    inc="pe")
                            mm_last = mm_done[(nk - 1) % NB]
                            eh_war[epar] = p.cnt["ve"]
                            if h + 4 < H:
                                _eh_load(h + 4)
                            pending[0] = (lambda hh=h, ml=mm_last: _normalize(hh, ml))
                        pending[0]()
                        pending[0] = None
                        p.wait("scalar", "ve", p.cnt["ve"])
                        p.dma("scalar", attnT_park[slot][:, 0:H * 256],
                              attnT[:, 0:H * 256], "dpark")
                        p.dma("scalar", attnT_park[slot][:, H * 256:H * 512],
                              attnT[:, H * 256:H * 512], "dpark")
                        p.tap(f"attnT_{slot}", _mk_tap(attnT), [("ve", p.cnt["ve"])])
                    p.barrier(bar_scratch)

        if phase_limit < 2:
            _emit_dummy_out(p, nc, out_d, bar_scratch)
            p.finalize(e(nc.Block()))
            return nc, p

        TB = 1024
        y1T_blk = e(nc.sbuf_tensor("y1T_blk", [128, NDC * TB], BF16))
        with ExitStack() as s2:
            e2 = s2.enter_context
            wo_sb = e2(nc.sbuf_tensor("wo_sb", [128, NDC * D], BF16))
            attnT_e = [e2(nc.sbuf_tensor(f"attnT_e{i}", [128, H * 512], BF16)) for i in range(2)]
            xres = [e2(nc.sbuf_tensor(f"xres{i}", [128, D], F32)) for i in range(2)]
            y1pre = [e2(nc.sbuf_tensor(f"y1pre{i}", [128, D], F32)) for i in range(2)]
            y1f = [e2(nc.sbuf_tensor(f"y1f{i}", [128, D], F32)) for i in range(2)]
            y1bf = [e2(nc.sbuf_tensor(f"y1bf{i}", [128, D], BF16)) for i in range(2)]
            y1T_e = e2(nc.sbuf_tensor("y1T_e", [128, NDC * 512], BF16))
            st = {
                "sum": e2(nc.sbuf_tensor("st_sum", [128, 1], F32)),
                "sumsq": e2(nc.sbuf_tensor("st_sumsq", [128, 1], F32)),
                "mean": e2(nc.sbuf_tensor("st_mean", [128, 1], F32)),
                "t1": e2(nc.sbuf_tensor("st_t1", [128, 1], F32)),
                "m2": e2(nc.sbuf_tensor("st_m2", [128, 1], F32)),
                "rv": e2(nc.sbuf_tensor("st_rv", [128, 1], F32)),
                "rstd": e2(nc.sbuf_tensor("st_rstd", [128, 1], F32)),
            }
            psY = [e2(nc.psum_tensor(f"psY{i}", [128, 512], F32)) for i in range(4)]
            psT = [e2(nc.psum_tensor(f"psT2{i}", [128, 512], BF16)) for i in range(2)]

            wov = wo_sb[:].rearrange("p (dc c) -> p dc c", c=D)
            for wi in range(8):
                p.dma("gpsimd", wov[:, wi * 2:(wi + 1) * 2, :],
                      wo_bf[:, wi * 2:(wi + 1) * 2, :], "ldg")
            wo_ldg = p.cnt["ldg"]
            xres_free = [0, 0]
            psY_free = 0
            y1T_parked = [0]
            y1_park_at = [0, 0]
            y1_tr_done = [0, 0]
            pe_state = {"psT_free": [0, 0]}
            aT_at = [0, 0]
            aT_free = [0, 0]

            def _at_load(slot):
                par = slot % 2
                p.wait("gpsimd", "pe", aT_free[par])
                at = 0
                for st in range(8):
                    at = p.dma("gpsimd",
                               attnT_e[par][:, st * 1024:(st + 1) * 1024],
                               attnT_park[slot][:, st * 1024:(st + 1) * 1024],
                               "dat")
                aT_at[par] = at
            pending_tr = [None]

            def _flush_tr():
                if pending_tr[0] is not None:
                    fn = pending_tr[0]
                    pending_tr[0] = None
                    fn()

            _at_load(0)
            for slot in range(NSLOT):
                par2 = slot % 2
                if slot + 1 < NSLOT:
                    _at_load(slot + 1)
                aTv = attnT_e[par2][:].rearrange("p (dc t) -> p dc t", t=512)
                if slot < 2:
                    y1Tv = (y1T_blk[:].rearrange("p (dc t) -> p dc t", t=TB)
                            [:, :, slot * 512:(slot + 1) * 512])
                else:
                    y1Tv = y1T_e[:].rearrange("p (dc t) -> p dc t", t=512)
                p.wait("tensor", "dat", aT_at[par2])
                p.wait("tensor", "ldg", wo_ldg)
                for ts in range(4):
                    par = ts % 2
                    p.wait("sync", "ve", xres_free[par])
                    xr_at = p.dma("sync", xres[par][:],
                                  xo[(slot * 4 + ts) * 128:(slot * 4 + ts + 1) * 128, :],
                                  f"dxr{par}")
                    p.wait("tensor", "ve", psY_free)
                    for dc in range(NDC):
                        for do in range(4):
                            p.op("tensor",
                                 lambda E, o=psY[do][:], l=aTv[:, dc, ts * 128:(ts + 1) * 128],
                                 r=wov[:, dc, do * 512:(do + 1) * 512],
                                 s=(dc == 0), t=(dc == NDC - 1):
                                 E.matmul(o, l, r, start=s, stop=t),
                                 inc="pe" if (dc == NDC - 1 and do == 3) else None)
                    mmc = p.cnt["pe"]
                    _flush_tr()
                    p.wait("vector", "pe", mmc)
                    p.wait("vector", f"dxr{par}", xr_at)
                    p.wait("vector", f"dy{par}", y1_park_at[par])
                    p.wait("vector", "pe", y1_tr_done[par])
                    for do in range(4):
                        p.op("vector",
                             lambda E, o=y1pre[par][:, do * 512:(do + 1) * 512],
                             a=psY[do][:],
                             b=xres[par][:, do * 512:(do + 1) * 512]:
                             E.tensor_tensor(o, a, b, ALU.add),
                             inc="ve" if do == 3 else None)
                    psY_free = p.cnt["ve"]
                    xres_free[par] = psY_free
                    c = emit_ln_stats_normalize(p, y1pre[par][:], y1f[par][:], st,
                                                1.0 / D, junk=y1f[par][:])
                    cbf = p.op("vector", lambda E, o=y1bf[par][:], i=y1f[par][:]:
                               E.tensor_copy(o, i), inc="ve")
                    p.wait("scalar", "ve", c)
                    y1_park_at[par] = p.dma(
                        "scalar", y1_park[(slot * 4 + ts) * 128:(slot * 4 + ts + 1) * 128, :],
                        y1f[par][:], f"dy{par}")

                    def _tr(par=par, ts=ts, sl=slot, cbf=cbf, yv=y1Tv):
                        p.wait("tensor", "ve", cbf)
                        if sl >= 2:
                            p.wait("tensor", "dy1t", y1T_parked[0])
                        emit_transpose_tile(p, psT, ident_bf, y1bf[par], yv, ts,
                                            pe_state)
                        y1_tr_done[par] = p.cnt["pe"]
                        if ts == 3 and sl >= 2:
                            p.wait("gpsimd", "se", p.cnt["se"])
                            p.wait("gpsimd", "pe", p.cnt["pe"])
                            pv = y1T_park[sl].rearrange("p (dc t) -> p dc t", t=512)
                            at = 0
                            for q4 in range(4):
                                at = p.dma("gpsimd", pv[:, q4 * 4:(q4 + 1) * 4, :],
                                           yv[:, q4 * 4:(q4 + 1) * 4, :], "dy1t")
                            y1T_parked[0] = at
                    pending_tr[0] = _tr
                aT_free[par2] = p.cnt["pe"]
            _flush_tr()
            p.barrier(bar_scratch)

        if phase_limit < 3:
            _emit_dummy_out(p, nc, out_d, bar_scratch)
            p.finalize(e(nc.Block()))
            return nc, p

        with ExitStack() as s3:
            e3 = s3.enter_context
            y_acc = e3(nc.sbuf_tensor("y_acc", [128, 8 * D], F32))
            h_sb = e3(nc.sbuf_tensor("h_sb", [128, HC * TB], BF16))
            wd_sb = [e3(nc.sbuf_tensor(f"wd_sb{i}", [128, HC * 1024], BF16)) for i in range(2)]
            wgc = [e3(nc.sbuf_tensor(f"wgc{i}", [128, NDC * 128], BF16)) for i in range(4)]
            wuc = [e3(nc.sbuf_tensor(f"wuc{i}", [128, NDC * 128], BF16)) for i in range(4)]
            sg_half = [e3(nc.sbuf_tensor(f"sgh{i}", [128, 512], BF16)) for i in range(2)]
            ctok = e3(nc.sbuf_tensor("ctok", [128, 64], F32))
            gat = {
                "eu": e3(nc.sbuf_tensor("g_eu", [128, 8], F32)),
                "gsum": e3(nc.sbuf_tensor("g_gsum", [128, 1], F32)),
                "rg": e3(nc.sbuf_tensor("g_rg", [128, 1], F32)),
                "gates": e3(nc.sbuf_tensor("g_gates", [128, 8], F32)),
                "m1": e3(nc.sbuf_tensor("g_m1", [128, 1], F32)),
                "mlt": e3(nc.sbuf_tensor("g_mlt", [128, 8], F32)),
                "g2": e3(nc.sbuf_tensor("g_g2", [128, 8], F32)),
                "m2": e3(nc.sbuf_tensor("g_m2", [128, 1], F32)),
                "keep": e3(nc.sbuf_tensor("g_keep", [128, 8], F32)),
                "den": e3(nc.sbuf_tensor("g_den", [128, 1], F32)),
                "rden": e3(nc.sbuf_tensor("g_rden", [128, 1], F32)),
            }
            st8 = {k: e3(nc.sbuf_tensor(f"s3_{k}", [128, 8], F32))
                   for k in ("sum", "sumsq", "mean", "t1", "m2", "rv", "rstd")}
            y2junk = e3(nc.sbuf_tensor("y2junk", [128, D], F32))

            y1Tb = y1T_blk[:].rearrange("p (dc t) -> p dc t", t=TB)
            yav = y_acc[:].rearrange("p (s d) -> p s d", d=D)
            hv = h_sb[:].rearrange("p (hc t) -> p hc t", t=TB)
            wdv = [wd_sb[i][:].rearrange("p (hc d) -> p hc d", d=1024) for i in range(2)]
            rwv = rw_sb[:].rearrange("p (dc e) -> p dc e", e=8)


            psR = [e3(nc.psum_tensor(f"psR{i}", [128, 8], F32)) for i in range(1)]
            psG = [e3(nc.psum_tensor(f"psG{i}", [128, 512], F32)) for i in range(2)]
            psU = [e3(nc.psum_tensor(f"psU{i}", [128, 512], F32)) for i in range(2)]
            psW = [e3(nc.psum_tensor(f"psW{i}", [128, 512], F32)) for i in range(3)]
            wgc_free = [p.cnt["pe"]] * 4
            wuc_free = [p.cnt["pe"]] * 4
            wd_free = [p.cnt["pe"], p.cnt["pe"]]
            hs_free = [p.cnt["pe"]] * HC
            psR_free = [p.cnt["se"]]
            psG_free = [p.cnt["se"], p.cnt["se"]]
            psU_free = [p.cnt["ve"], p.cnt["ve"]]
            psW_free = [p.cnt["ve"]] * 3
            ctsb_done = 0
            yadd_last = 0
            gu_last_pe = 0
            y1t_at = [0, 0]
            yacc_at = [0, 0]
            NUNIT = NE + 2

            def _unit_srcs(unit):
                if unit < NE:
                    return (wg_p[unit], wu_p[unit],
                            wd_r[unit].rearrange("hc p d -> p hc d"))
                sh = unit - NE
                return (sg_p[sh * HC:(sh + 1) * HC], su_p[sh * HC:(sh + 1) * HC],
                        sd_r[sh * HC:(sh + 1) * HC].rearrange("hc p d -> p hc d"))

            NCHUNK = 2 * NUNIT * HC
            gu_at = {}

            def _issue_gu(gidx):
                if gidx >= NCHUNK:
                    return
                u, hc = divmod(gidx % (NUNIT * HC), HC)
                par = gidx % 4
                wg_src, wu_src, _ = _unit_srcs(u)
                p.wait("gpsimd", "pe", wgc_free[par])
                ga = p.dma("gpsimd",
                           wgc[par][:].rearrange("p (dc j) -> p dc j", j=128),
                           wg_src[hc], f"dg{par % 2}")
                p.wait("gpsimd", "pe", wuc_free[par])
                ua = p.dma("gpsimd",
                           wuc[par][:].rearrange("p (dc j) -> p dc j", j=128),
                           wu_src[hc], f"du{par % 2}")
                gu_at[gidx] = (par, ga, ua)

            def _y1t_load(bb, engs, pe_gate):
                for eng in engs:
                    p.wait(eng, "pe", pe_gate)
                at = 0
                k = 0
                for half in range(2):
                    slot = bb * 2 + half
                    dst = (y1T_blk[:].rearrange("p (dc t) -> p dc t", t=TB)
                           [:, :, half * 512:(half + 1) * 512])
                    src = y1T_park[slot].rearrange("p (dc t) -> p dc t", t=512)
                    for q4 in range(4):
                        at = p.dma(engs[k % len(engs)],
                                   dst[:, q4 * 4:(q4 + 1) * 4, :],
                                   src[:, q4 * 4:(q4 + 1) * 4, :], "lds")
                        k += 1
                y1t_at[bb] = at

            def _yacc_load(bb, engs):
                for eng in engs:
                    p.wait(eng, "ve", p.cnt["ve"])
                    p.wait(eng, "po", p.cnt["po"])
                    p.wait(eng, "do0", p.cnt["do0"])
                    p.wait(eng, "do1", p.cnt["do1"])
                at = 0
                for st in range(4):
                    at = p.dma(engs[st % len(engs)], yav[:, st * 2:(st + 1) * 2, :],
                               y1_park[bb * TB + st * 256:bb * TB + (st + 1) * 256, :]
                               .rearrange("(s p) d -> p s d", p=128), "lds")
                yacc_at[bb] = at

            y1t_at[0] = 0
            _yacc_load(0, ("sync", "scalar"))
            for g0 in range(4):
                _issue_gu(g0)

            ctsb_hold = [0]
            pending_norm = []

            def _emit_router_ts(blkb, ts):
                    rp = 0
                    p.wait("tensor", "se", psR_free[rp])
                    for dc in range(NDC):
                        p.op("tensor",
                             lambda E, o=psR[rp][:], l=y1Tb[:, dc, ts * 128:(ts + 1) * 128],
                             r=rwv[:, dc, :], s=(dc == 0), t=(dc == NDC - 1):
                             E.matmul(o, l, r, start=s, stop=t),
                             inc="pe" if dc == NDC - 1 else None)
                    rmm = p.cnt["pe"]
                    p.wait("scalar", "pe", rmm)
                    p.wait("scalar", "ve", ctsb_hold[0])
                    c = p.op("scalar", lambda E, o=gat["eu"][:], i=psR[rp][:]:
                             E.activation(out=o, in_=i, func=AF.Exp), inc="se")
                    psR_free[rp] = c
                    p.wait("vector", "se", c)
                    p.op("vector", lambda E, o=gat["gsum"][:], i=gat["eu"][:]:
                         E.reduce_sum(o, i, axis=AX.X))
                    p.op("vector", lambda E, o=gat["rg"][:], i=gat["gsum"][:]:
                         E.reciprocal(o, i))
                    p.op("vector", lambda E, o=gat["gates"][:], i=gat["eu"][:],
                         s1=gat["rg"][:]:
                         E.tensor_scalar(o, i, s1, None, ALU.mult))
                    p.op("vector", lambda E, o=gat["m1"][:], i=gat["gates"][:]:
                         E.reduce_max(o, i, axis=AX.X))
                    p.op("vector", lambda E, o=gat["mlt"][:], i=gat["gates"][:],
                         s1=gat["m1"][:]:
                         E.tensor_scalar(o, i, s1, None, ALU.is_lt))
                    p.op("vector", lambda E, o=gat["g2"][:], a=gat["gates"][:],
                         b=gat["mlt"][:]:
                         E.tensor_tensor(o, a, b, ALU.mult))
                    p.op("vector", lambda E, o=gat["m2"][:], i=gat["g2"][:]:
                         E.reduce_max(o, i, axis=AX.X))
                    p.op("vector", lambda E, o=gat["keep"][:], i=gat["gates"][:],
                         s1=gat["m2"][:]:
                         E.tensor_scalar(o, i, s1, None, ALU.is_ge))
                    p.op("vector", lambda E, o=gat["den"][:], i=gat["m1"][:],
                         s1=gat["m2"][:]:
                         E.tensor_scalar(o, i, s1, EPS_GATE, ALU.add, ALU.add))
                    p.op("vector", lambda E, o=gat["rden"][:], i=gat["den"][:]:
                         E.reciprocal(o, i))
                    ctsb_hold[0] = p.op(
                        "vector",
                        lambda E, o=ctok[:, ts * 8:(ts + 1) * 8], i=gat["gates"][:],
                        s1=gat["rden"][:], b=gat["keep"][:]:
                        E.scalar_tensor_tensor(o, i, s1, b, ALU.mult, ALU.mult),
                        inc="ve")


            for blkb in range(2):
                with ExitStack() as s3p:
                    e3p = s3p.enter_context
                    p.wait("tensor", "lds", y1t_at[blkb])
                    p.wait("tensor", "ldg", p.cnt["ldg"])

                    for unit in range(NUNIT):
                        routed = unit < NE
                        _, _, wd_src = _unit_srcs(unit)
                        wd_at = [0, 0]
                        for dhh in range(2):
                            p.wait("gpsimd", "pe", wd_free[dhh])
                            p.dma("gpsimd", wdv[dhh][:, 0:6, :],
                                  wd_src[:, 0:6, dhh * 1024:(dhh + 1) * 1024],
                                  f"dwd{dhh}")
                            wd_at[dhh] = p.dma("gpsimd", wdv[dhh][:, 6:HC, :],
                                               wd_src[:, 6:HC, dhh * 1024:(dhh + 1) * 1024],
                                               f"dwd{dhh}")
                        for hc in range(HC):
                            gidx = (blkb * NUNIT + unit) * HC + hc
                            par, ga, ua = gu_at.pop(gidx)
                            wgv = wgc[par][:].rearrange("p (dc j) -> p dc j", j=128)
                            wuv = wuc[par][:].rearrange("p (dc j) -> p dc j", j=128)
                            p.wait("tensor", f"dg{par % 2}", ga)
                            p.wait("tensor", f"du{par % 2}", ua)
                            for th in range(2):
                                tpar = th
                                p.wait("tensor", "se", psG_free[tpar])
                                for dc in range(NDC):
                                    p.op("tensor",
                                         lambda E, o=psG[tpar][:], l=wgv[:, dc, :],
                                         r=y1Tb[:, dc, th * 512:(th + 1) * 512],
                                         s=(dc == 0), t=(dc == NDC - 1):
                                         E.matmul(o, l, r, start=s, stop=t),
                                         inc="pe" if dc == NDC - 1 else None)
                                gmm = p.cnt["pe"]
                                p.wait("tensor", "ve", psU_free[tpar])
                                for dc in range(NDC):
                                    p.op("tensor",
                                         lambda E, o=psU[tpar][:], l=wuv[:, dc, :],
                                         r=y1Tb[:, dc, th * 512:(th + 1) * 512],
                                         s=(dc == 0), t=(dc == NDC - 1):
                                         E.matmul(o, l, r, start=s, stop=t),
                                         inc="pe" if dc == NDC - 1 else None)
                                umm = p.cnt["pe"]
                                p.wait("scalar", "pe", gmm)
                                p.wait("scalar", "ve", psU_free[tpar])
                                psG_free[tpar] = p.op(
                                    "scalar",
                                    lambda E, o=sg_half[tpar][:], i=psG[tpar][:]:
                                    E.activation(out=o, in_=i, func=AF.Silu), inc="se")
                                p.wait("vector", "pe", hs_free[hc])
                                p.wait("vector", "se", psG_free[tpar])
                                p.wait("vector", "pe", umm)
                                psU_free[tpar] = p.op(
                                    "vector",
                                    lambda E, o=hv[:, hc, th * 512:(th + 1) * 512],
                                    a=psU[tpar][:], b=sg_half[tpar][:]:
                                    E.tensor_tensor(o, a, b, ALU.mult), inc="ve")
                            wgc_free[par] = p.cnt["pe"]
                            wuc_free[par] = p.cnt["pe"]
                            _issue_gu(gidx + 4)
                            if unit == 0 and hc < 8:
                                _emit_router_ts(blkb, hc)
                                if pending_norm:
                                    pending_norm.pop(0)()
                        gu_last_pe = p.cnt["pe"]
                        h_done = p.cnt["ve"]
                        p.wait("vector", "lds", yacc_at[blkb])
                        p.wait("tensor", "ve", h_done)
                        for dhh in range(2):
                            p.wait("tensor", f"dwd{dhh}", wd_at[dhh])
                            for ts in range(8):
                                for do2 in range(2):
                                    do = dhh * 2 + do2
                                    wpar = (ts * 2 + do2) % 3
                                    p.wait("tensor", "ve", psW_free[wpar])
                                    for hc in range(HC):
                                        p.op("tensor",
                                             lambda E, o=psW[wpar][:],
                                             l=hv[:, hc, ts * 128:(ts + 1) * 128],
                                             r=wdv[dhh][:, hc, do2 * 512:(do2 + 1) * 512],
                                             s=(hc == 0), t=(hc == HC - 1):
                                             E.matmul(o, l, r, start=s, stop=t),
                                             inc="pe" if hc == HC - 1 else None)
                                    wmm = p.cnt["pe"]
                                    p.wait("vector", "pe", wmm)
                                    if routed:
                                        p.wait("vector", "ve", ctsb_hold[0])
                                        psW_free[wpar] = p.op(
                                            "vector",
                                            lambda E, o=yav[:, ts, do * 512:(do + 1) * 512],
                                            i=psW[wpar][:],
                                            sc=ctok[:, ts * 8 + unit:ts * 8 + unit + 1],
                                            b=yav[:, ts, do * 512:(do + 1) * 512]:
                                            E.scalar_tensor_tensor(o, i, sc, b,
                                                                   ALU.mult, ALU.add),
                                            inc="ve")
                                    else:
                                        psW_free[wpar] = p.op(
                                            "vector",
                                            lambda E, o=yav[:, ts, do * 512:(do + 1) * 512],
                                            a=yav[:, ts, do * 512:(do + 1) * 512],
                                            b=psW[wpar][:]:
                                            E.tensor_tensor(o, a, b, ALU.add), inc="ve")
                                    if unit == NUNIT - 1 and dhh == 1 and do2 == 1:
                                        p.op("vector",
                                             lambda E, o=st8["sum"][:, ts:ts + 1],
                                             i=yav[:, ts, :]:
                                             E.reduce_sum(o, i, axis=AX.X))
                                        p.op("vector", lambda E, o=y2junk[:],
                                             i=yav[:, ts, :],
                                             a=st8["sumsq"][:, ts:ts + 1]:
                                             E.scalar_tensor_tensor(
                                                 o, i, 1.0, i, ALU.mult, ALU.mult,
                                                 accum_out=a))
                            wd_free[dhh] = p.cnt["pe"]
                        yadd_last = p.cnt["ve"]
                        for hc in range(HC):
                            hs_free[hc] = p.cnt["pe"]
                    if blkb == 0:
                        _y1t_load(1, ("sync",), gu_last_pe)
                    p.op("vector", lambda E, o=st8["mean"][:], i=st8["sum"][:]:
                         E.tensor_scalar(o, i, 1.0 / D, None, ALU.mult))
                    p.op("vector", lambda E, o=st8["t1"][:], i=st8["sumsq"][:]:
                         E.tensor_scalar(o, i, 1.0 / D, None, ALU.mult))
                    p.op("vector", lambda E, o=st8["m2"][:], a=st8["mean"][:]:
                         E.tensor_tensor(o, a, a, ALU.mult))
                    p.op("vector", lambda E, o=st8["t1"][:], a=st8["t1"][:],
                         b=st8["m2"][:]: E.tensor_tensor(o, a, b, ALU.subtract))
                    p.op("vector", lambda E, o=st8["t1"][:], i=st8["t1"][:]:
                         E.tensor_scalar(o, i, EPS_LN, None, ALU.add))
                    cstat = p.op("vector", lambda E, o=st8["rv"][:], i=st8["t1"][:]:
                                 E.reciprocal(o, i), inc="ve")
                    p.wait("scalar", "ve", cstat)
                    csq = p.op("scalar", lambda E, o=st8["rstd"][:], i=st8["rv"][:]:
                               E.activation(out=o, in_=i, func=AF.Sqrt), inc="se")

                    def _norm_store(ts, bb=blkb, csq_=csq):
                        par = ts % 2
                        p.wait("vector", "se", csq_)
                        c = p.op("vector",
                                 lambda E, o=yav[:, ts, :], i=yav[:, ts, :],
                                 m=st8["mean"][:, ts:ts + 1], r=st8["rstd"][:, ts:ts + 1]:
                                 E.tensor_scalar(o, i, m, r, ALU.subtract, ALU.mult),
                                 inc="ve")
                        p.wait("sync", "ve", c)
                        p.dma(
                            "sync",
                            out_d[(bb * 8 + ts) * 128:(bb * 8 + ts + 1) * 128, :],
                            yav[:, ts, :], f"do{par}")
                        if bb == 0 and ts % 2 == 1:
                            st = ts // 2
                            p.wait("sync", "do0", p.cnt["do0"])
                            p.wait("sync", "do1", p.cnt["do1"])
                            yacc_at[1] = p.dma(
                                "sync", yav[:, st * 2:(st + 1) * 2, :],
                                y1_park[TB + st * 256:TB + (st + 1) * 256, :]
                                .rearrange("(s p) d -> p s d", p=128), "lds")
                    if blkb == 0:
                        pending_norm[:] = [lambda ts=ts, f=_norm_store: f(ts)
                                           for ts in range(8)]
                    else:
                        for ts in range(8):
                            _norm_store(ts)
            p.barrier(bar_scratch)

        p.finalize(e(nc.Block()))
        return nc, p


def _emit_dummy_out(p, nc, out_d, bar_scratch):
    z = nc.sbuf_tensor("zline", [128, D], F32)
    with z as zt:
        p.op("vector", lambda E: E.memset(zt[:], 0.0), inc="ve")
        p.wait("sync", "ve", p.cnt["ve"])
        for i in range(NSLOT * 512 // 128):
            p.dma("sync", out_d[i * 128:(i + 1) * 128, :], zt[:], "dpark")
        p.wait("sync", "dpark", p.cnt["dpark"])
        p.barrier(bar_scratch)



def _host_tables():
    half = DH // 2
    theta = 1.0 / ROPE_BASE ** (np.arange(half, dtype=np.float64) / half)
    pos = np.arange(T, dtype=np.float64)
    fr = np.outer(theta, pos)
    csf = np.concatenate([np.cos(fr), np.sin(fr)], axis=0).astype(np.float32)
    ident = np.eye(128, dtype=np.float32)
    return csf, ident


def _etab_for_role(r):
    p = np.arange(128, dtype=np.int64)[:, None]
    y = np.arange(YT, dtype=np.int64)[None, :]
    v = (p - y + (CPRIME - r * 512)).astype(np.float64)
    slopes = 2.0 ** (-8.0 * np.arange(1, H + 1, dtype=np.float64) / H)
    out = np.zeros((H, 128, YT), dtype=ml_dtypes.bfloat16)
    with np.errstate(under="ignore"):
        for h in range(H):
            e = np.where(v <= 0, np.exp(slopes[h] * np.minimum(v, 0.0)), 0.0)
            out[h] = e.astype(ml_dtypes.bfloat16)
    return out


def kernel(L0_attn_x, L0_attn_Wq, L0_attn_Wk, L0_attn_Wv, L0_attn_Wo,
           L0_ffn_router_w, L0_ffn_wg, L0_ffn_wu, L0_ffn_wd,
           L0_ffn_sg, L0_ffn_su, L0_ffn_sd):
    x = np.ascontiguousarray(L0_attn_x, dtype=np.float32)

    wq_p = np.ascontiguousarray(
        L0_attn_Wq.reshape(NDC, 128, 16, 128).transpose(1, 0, 2, 3))
    wk_p = np.ascontiguousarray(L0_attn_Wk.reshape(NDC, 128, 512).transpose(1, 0, 2))
    wv_p = np.ascontiguousarray(L0_attn_Wv.reshape(NDC, 128, 512).transpose(1, 0, 2))
    wo_p = np.ascontiguousarray(L0_attn_Wo.reshape(NDC, 128, D).transpose(1, 0, 2))
    rw_p = np.ascontiguousarray(L0_ffn_router_w.T.reshape(NDC, 128, 8).transpose(1, 0, 2))
    wg_p = np.ascontiguousarray(
        L0_ffn_wg.reshape(NE, NDC, 128, HC, 128).transpose(0, 3, 2, 1, 4))
    wu_p = np.ascontiguousarray(
        L0_ffn_wu.reshape(NE, NDC, 128, HC, 128).transpose(0, 3, 2, 1, 4))
    wd_r = np.ascontiguousarray(L0_ffn_wd.reshape(NE, HC, 128, D))
    sg_p = np.ascontiguousarray(
        L0_ffn_sg.reshape(NDC, 128, 2 * HC, 128).transpose(2, 1, 0, 3))
    su_p = np.ascontiguousarray(
        L0_ffn_su.reshape(NDC, 128, 2 * HC, 128).transpose(2, 1, 0, 3))
    sd_r = np.ascontiguousarray(L0_ffn_sd.reshape(2 * HC, 128, D))

    csf, ident = _host_tables()
    etab_role = [_etab_for_role(0), _etab_for_role(1)]

    nc, _ = build_program()

    in_maps = []
    own_rows = {}
    for c in range(8):
        b, r = c // 2, c % 2
        slots = [2 * i + r for i in range(NSLOT)]
        own_rows[c] = slots
        rows = np.concatenate([np.arange(s * 512, (s + 1) * 512) for s in slots])
        xo = np.ascontiguousarray(x[b][rows])
        cso = np.ascontiguousarray(csf[:, rows])
        in_maps.append({
            "xf": x[b], "xo": xo,
            "wq_p": wq_p, "wk_p": wk_p, "wv_p": wv_p, "wo_p": wo_p, "rw_p": rw_p,
            "wg_p": wg_p, "wu_p": wu_p, "wd_r": wd_r,
            "sg_p": sg_p, "su_p": su_p, "sd_r": sd_r,
            "csf": csf, "cso": cso, "etab": etab_role[r],
            "ident": ident,
        })

    res = run_bass_kernel_spmd(nc, in_maps, core_ids=list(range(8)))

    out = np.empty((B, T, D), dtype=np.float32)
    for c in range(8):
        b = c // 2
        oc = np.asarray(res.results[c]["out"])
        for i, s in enumerate(own_rows[c]):
            out[b, s * 512:(s + 1) * 512, :] = oc[i * 512:(i + 1) * 512, :]
    return out




# version 19
# speedup vs baseline: 1.0679x; 1.0048x over previous
import numpy as np
import ml_dtypes
from contextlib import ExitStack

import concourse.bass as bass
import concourse.mybir as mybir
from concourse.bass_utils import run_bass_kernel_spmd

F32 = mybir.dt.float32
BF16 = mybir.dt.bfloat16
AF = mybir.ActivationFunctionType
ALU = mybir.AluOpType
AX = mybir.AxisListType

B, T, D = 4, 4096, 2048
H, G, DH = 16, 4, 128
NE, HF, HS = 8, 1408, 2816
NDC = D // 128
HC = HF // 128
ROPE_BASE = 10000.0
SCALE = 1.0 / np.sqrt(DH)
EXP_BIAS = -8.0
EPS_LN = 1e-5
EPS_GATE = 1e-20
NSLOT = 4
YT = 4480
CPRIME = 896
MASK_NEG = -60000.0

DBG_TAPS = set()
P0_BLOCKS = 8

ENGINES = ("tensor", "vector", "scalar", "sync", "gpsimd")


class Prog:

    def __init__(self, nc):
        self.nc = nc
        self.q = {e: [] for e in ENGINES}
        self.sems = {}
        self.cnt = {}
        self.last_wait = {e: {} for e in ENGINES}
        self.bar_k = 0
        self.taps = {}

    def mksem(self, stack, *names):
        for n in names:
            self.sems[n] = stack.enter_context(self.nc.semaphore(n))
            self.cnt[n] = 0

    def mkdmasem(self, stack, eng, *names):
        if not hasattr(self, "dma_sems"):
            self.dma_sems = []
        for n in names:
            self.sems[n] = stack.enter_context(self.nc.semaphore(n))
            self.cnt[n] = 0
            self.dma_sems.append((eng, n))

    AUTO_SEM = {"vector": "ve", "scalar": "se"}

    def op(self, eng, build, inc=None, amt=1):
        auto = self.AUTO_SEM.get(eng)
        if auto is not None:
            self.wait(eng, auto, self.cnt[auto])
            assert inc is None or inc == auto
            inc = auto
        semh = self.sems[inc] if inc else None
        if inc:
            self.cnt[inc] += amt

        def go(E, build=build, semh=semh, amt=amt):
            ins = build(E)
            if semh is not None:
                ins.then_inc(semh, amt)
        self.q[eng].append(go)
        return self.cnt[inc] if inc else None

    def op_relaxed(self, eng, build, amt=1):
        auto = self.AUTO_SEM[eng]
        self.cnt[auto] += amt
        semh = self.sems[auto]

        def go(E, build=build, semh=semh, amt=amt):
            build(E).then_inc(semh, amt)
        self.q[eng].append(go)
        return self.cnt[auto]

    def wait(self, eng, sem, val=None):
        if val is None:
            val = self.cnt[sem]
        if val <= 0:
            return
        lw = self.last_wait[eng]
        if lw.get(sem, -1) >= val:
            return
        lw[sem] = val
        semh = self.sems[sem]
        self.q[eng].append(lambda E, s=semh, v=val: E.wait_ge(s, v))

    def dma(self, eng, out, in_, sem):
        semh = self.sems[sem]
        self.cnt[sem] += 16

        def go(E, out=out, in_=in_, semh=semh):
            E.dma_start(out=out, in_=in_).then_inc(semh, 16)
        self.q[eng].append(go)
        return self.cnt[sem]

    def barrier(self, bar_scratch=None):
        for e in ENGINES:
            for n in ("pe", "ve", "se", "po"):
                self.wait(e, n)
            for _eng, n in getattr(self, "dma_sems", []):
                self.wait(e, n)

    def tap(self, name, ap_fn, waits):
        if name not in DBG_TAPS:
            return
        shape = ap_fn("shape")
        dt = ap_fn("dtype")
        t = self.nc.dram_tensor("tap_" + name, shape, dt, kind="ExternalOutput").ap()
        self.taps[name] = True
        for sem, val in waits:
            self.wait("sync", sem, val)
        self.dma("sync", t, ap_fn("ap"), "dt")

    def finalize(self, block):
        block.gpsimd(lambda E: [f(E) for f in self.q["gpsimd"]])
        block.sync(lambda E: [f(E) for f in self.q["sync"]])
        block.tensor(lambda E: [f(E) for f in self.q["tensor"]])
        block.vector(lambda E: [f(E) for f in self.q["vector"]])
        block.scalar(lambda E: [f(E) for f in self.q["scalar"]])


def _mk_tap(tile, sl=None):
    def fn(what):
        ap = tile[:] if sl is None else sl
        if what == "shape":
            return list(ap.shape)
        if what == "dtype":
            return ap.dtype
        return ap
    return fn



def emit_transpose_tile(p, psT, ident_bf, src_tile, dst_view, tt, pe_state):
    for g4 in range(4):
        par = (tt * 4 + g4) % 2
        p.wait("tensor", "se", pe_state["psT_free"][par])
        for j in range(4):
            dc = g4 * 4 + j
            p.op("tensor",
                 lambda E, o=psT[par][:, j * 128:(j + 1) * 128],
                 i=src_tile[:, dc * 128:(dc + 1) * 128], idn=ident_bf[:]:
                 E.transpose(o, i, idn),
                 inc="pe" if j == 3 else None)
        done = p.cnt["pe"]
        p.wait("scalar", "pe", done)
        out = dst_view[:, g4 * 4:(g4 + 1) * 4, tt * 128:(tt + 1) * 128]
        src = psT[par][:].rearrange("p (j t) -> p j t", t=128)
        c = p.op("scalar", lambda E, o=out, i=src: E.copy(o, i), inc="se")
        pe_state["psT_free"][par] = c


def emit_rope(p, psQK, cs, cs_off, dst_lo, dst_hi, scr1, scr2, pe_count):
    p.wait("vector", "pe", pe_count)
    cosap = cs[0:64, cs_off:cs_off + 512]
    sinap = cs[64:128, cs_off:cs_off + 512]
    p.op("vector", lambda E, o=scr1[0:64, :], a=psQK[0:64, :], b=cosap:
         E.tensor_tensor(o, a, b, ALU.mult))
    p.op("vector", lambda E, o=scr2[0:64, :], a=psQK[64:128, :], b=sinap:
         E.tensor_tensor(o, a, b, ALU.mult))
    p.op("vector", lambda E, o=dst_lo, a=scr1[0:64, :], b=scr2[0:64, :]:
         E.tensor_tensor(o, a, b, ALU.subtract), inc="ve")
    p.op("vector", lambda E, o=scr1[0:64, :], a=psQK[0:64, :], b=sinap:
         E.tensor_tensor(o, a, b, ALU.mult))
    p.op("vector", lambda E, o=scr2[0:64, :], a=psQK[64:128, :], b=cosap:
         E.tensor_tensor(o, a, b, ALU.mult))
    return p.op("vector", lambda E, o=dst_hi, a=scr1[0:64, :], b=scr2[0:64, :]:
                E.tensor_tensor(o, a, b, ALU.add), inc="ve")


def emit_ln_stats_normalize(p, y_in, y_out, st, inv_n, junk):
    n = y_in.shape[-1]
    p.op("vector", lambda E, o=st["sum"][:], i=y_in: E.reduce_sum(o, i, axis=AX.X))
    p.op("vector", lambda E, o=junk, i=y_in, a=st["sumsq"][:]:
         E.scalar_tensor_tensor(o, i, 1.0, i, ALU.mult, ALU.mult, accum_out=a))
    p.op("vector", lambda E, o=st["mean"][:], i=st["sum"][:]:
         E.tensor_scalar(o, i, inv_n, None, ALU.mult))
    p.op("vector", lambda E, o=st["t1"][:], i=st["sumsq"][:]:
         E.tensor_scalar(o, i, inv_n, None, ALU.mult))
    p.op("vector", lambda E, o=st["m2"][:], a=st["mean"][:]:
         E.tensor_tensor(o, a, a, ALU.mult))
    p.op("vector", lambda E, o=st["t1"][:], a=st["t1"][:], b=st["m2"][:]:
         E.tensor_tensor(o, a, b, ALU.subtract))
    p.op("vector", lambda E, o=st["t1"][:], i=st["t1"][:]:
         E.tensor_scalar(o, i, EPS_LN, None, ALU.add))
    c = p.op("vector", lambda E, o=st["rv"][:], i=st["t1"][:]:
             E.reciprocal(o, i), inc="ve")
    p.wait("scalar", "ve", c)
    cs = p.op("scalar", lambda E, o=st["rstd"][:], i=st["rv"][:]:
              E.activation(out=o, in_=i, func=AF.Sqrt), inc="se")
    p.wait("vector", "se", cs)
    return p.op("vector", lambda E, o=y_out, i=y_in, m=st["mean"][:], r=st["rstd"][:]:
                E.tensor_scalar(o, i, m, r, ALU.subtract, ALU.mult), inc="ve")



def build_program(phase_limit=4):
    nc = bass.Bass("TRN2", target_bir_lowering=False)

    xf = nc.dram_tensor("xf", [T, D], F32, kind="ExternalInput").ap()
    xo = nc.dram_tensor("xo", [NSLOT * 512, D], F32, kind="ExternalInput").ap()
    etab = nc.dram_tensor("etab", [H, 128, YT], BF16, kind="ExternalInput").ap()
    wq_p = nc.dram_tensor("wq_p", [128, NDC, 16, 128], F32, kind="ExternalInput").ap()
    wk_p = nc.dram_tensor("wk_p", [128, NDC, 512], F32, kind="ExternalInput").ap()
    wv_p = nc.dram_tensor("wv_p", [128, NDC, 512], F32, kind="ExternalInput").ap()
    wo_p = nc.dram_tensor("wo_p", [128, NDC, D], F32, kind="ExternalInput").ap()
    rw_p = nc.dram_tensor("rw_p", [128, NDC, 8], F32, kind="ExternalInput").ap()
    wg_p = nc.dram_tensor("wg_p", [NE, HC, 128, NDC, 128], F32, kind="ExternalInput").ap()
    wu_p = nc.dram_tensor("wu_p", [NE, HC, 128, NDC, 128], F32, kind="ExternalInput").ap()
    wd_r = nc.dram_tensor("wd_r", [NE, HC, 128, D], F32, kind="ExternalInput").ap()
    sg_p = nc.dram_tensor("sg_p", [2 * HC, 128, NDC, 128], F32, kind="ExternalInput").ap()
    su_p = nc.dram_tensor("su_p", [2 * HC, 128, NDC, 128], F32, kind="ExternalInput").ap()
    sd_r = nc.dram_tensor("sd_r", [2 * HC, 128, D], F32, kind="ExternalInput").ap()
    csf = nc.dram_tensor("csf", [128, T], F32, kind="ExternalInput").ap()
    cso_in = nc.dram_tensor("cso", [128, NSLOT * 512], F32, kind="ExternalInput").ap()
    ident_in = nc.dram_tensor("ident", [128, 128], F32, kind="ExternalInput").ap()
    out_d = nc.dram_tensor("out", [NSLOT * 512, D], F32, kind="ExternalOutput").ap()

    attnT_park = nc.dram_tensor("attnT_park", [NSLOT, 128, H * 512], BF16, kind="Internal").ap()
    y1_park = nc.dram_tensor("y1_park", [NSLOT * 512, D], F32, kind="Internal").ap()
    y1T_park = nc.dram_tensor("y1T_park", [NSLOT, 128, NDC * 512], BF16, kind="Internal").ap()
    wq_bf = nc.dram_tensor("wq_bf", [128, NDC, 16, 128], BF16, kind="Internal").ap()
    wo_bf = nc.dram_tensor("wo_bf", [128, NDC, D], BF16, kind="Internal").ap()

    with ExitStack() as g:
        e = g.enter_context
        p = Prog(nc)
        p.mksem(g, "pe", "ve", "se", "po")
        p.mkdmasem(g, "gpsimd", "ldg", "dkv", "dxs", "dx0", "dx1", "dq0", "dq1",
                   "dg0", "dg1", "du0", "du1", "dwd0", "dwd1")
        p.mkdmasem(g, "sync", "lds", "det", "dxr0", "dxr1", "dat", "dpark", "dy1t", "dy0", "dy1", "do0", "do1", "dt")
        p.mkdmasem(g, "scalar", "dwi", "dwo")
        p.mkdmasem(g, "gpsimd", "dtg")

        ident_bf = e(nc.sbuf_tensor("ident_bf", [128, 128], BF16))
        ones_sq = e(nc.sbuf_tensor("ones_sq", [128, 128], BF16))
        expbias = e(nc.sbuf_tensor("expbias", [128, 1], F32))
        rw_sb = e(nc.sbuf_tensor("rw_sb", [128, NDC * 8], BF16))
        bar_scratch = None
        zeros_c = e(nc.sbuf_tensor("zeros_c", [128, 1], F32))
        ones_c = e(nc.sbuf_tensor("ones_c", [128, 1], F32))
        nc.const_aps.register(F32, 0.0, zeros_c[:])
        nc.const_aps.register(F32, 1.0, ones_c[:])
        p.op("vector", lambda E: E.memset(zeros_c[:], 0.0))
        p.op("vector", lambda E: E.memset(ones_c[:], 1.0))

        p.dma("gpsimd", ident_bf[:], ident_in, "ldg")
        p.dma("gpsimd", rw_sb[:], rw_p.rearrange("p dc e -> p (dc e)"), "ldg")
        p.op("vector", lambda E: E.memset(ones_sq[:], 1.0))
        p.op("vector", lambda E: E.memset(expbias[:], EXP_BIAS), inc="ve")
        consts_ve = p.cnt["ve"]
        p.wait("scalar", "ve", consts_ve)

        with ExitStack() as kv_era:
            ek = kv_era.enter_context
            K_sb = ek(nc.sbuf_tensor("K_sb", [128, G * T], BF16))
            V_sb = ek(nc.sbuf_tensor("V_sb", [128, (T // 128) * 512], BF16))
            Kv = K_sb[:].rearrange("p (g t) -> p g t", g=G)
            Vv = V_sb[:].rearrange("p (kc t) -> p kc t", t=512)

            with ExitStack() as s0:
                e0 = s0.enter_context
                csf_sb = e0(nc.sbuf_tensor("csf_sb", [128, T], F32))
                p.dma("sync", csf_sb[:], csf, "lds")
                wk_sb = e0(nc.sbuf_tensor("wk_sb", [128, NDC * 512], BF16))
                wv_sb = e0(nc.sbuf_tensor("wv_sb", [128, NDC * 512], BF16))
                x_st = [e0(nc.sbuf_tensor(f"x_st{i}", [128, 4 * D], BF16)) for i in range(2)]
                xT_blk = e0(nc.sbuf_tensor("xT_blk", [128, NDC * 512], BF16))
                rsc1 = e0(nc.sbuf_tensor("rsc1", [64, 512], F32))
                rsc2 = e0(nc.sbuf_tensor("rsc2", [64, 512], F32))
                psT = [e0(nc.psum_tensor(f"psT{i}", [128, 512], BF16)) for i in range(2)]
                psK = [e0(nc.psum_tensor(f"psK{i}", [128, 512], F32)) for i in range(4)]
                psV = [e0(nc.psum_tensor(f"psV{i}", [128, 512], F32)) for i in range(2)]

                p.dma("gpsimd", wk_sb[:].rearrange("p (dc c) -> p dc c", c=512), wk_p, "dkv")
                p.dma("gpsimd", wv_sb[:].rearrange("p (dc c) -> p dc c", c=512), wv_p, "dkv")
                w_dkv = p.cnt["dkv"]

                wqc_st = [e0(nc.sbuf_tensor("wqc_st0", [128, 8192], BF16))]
                wcast_out_at = [0]

                def _emit_cast(qq):
                    cpar = 0
                    if qq < 4:
                        src = wq_p[:, :, qq * 4:(qq + 1) * 4, :]
                        dstv = wqc_st[cpar][:].rearrange("p (dc h j) -> p dc h j", h=4, j=128)
                        dst = wq_bf[:, :, qq * 4:(qq + 1) * 4, :]
                    else:
                        q4 = qq - 4
                        src = wo_p[:, :, q4 * 512:(q4 + 1) * 512]
                        dstv = wqc_st[cpar][:].rearrange("p (dc c) -> p dc c", c=512)
                        dst = wo_bf[:, :, q4 * 512:(q4 + 1) * 512]
                    p.wait("gpsimd", "dwo", wcast_out_at[cpar])
                    at = p.dma("gpsimd", dstv, src, "dwi")
                    p.wait("sync", "dwi", at)
                    wcast_out_at[cpar] = p.dma("sync", dst, dstv, "dwo")
                wkv = wk_sb[:].rearrange("p (dc c) -> p dc c", c=512)
                wvv = wv_sb[:].rearrange("p (dc c) -> p dc c", c=512)
                xTv = xT_blk[:].rearrange("p (dc t) -> p dc t", t=512)

                p.wait("vector", "lds", p.cnt["lds"])
                p.wait("tensor", "ldg", p.cnt["ldg"])
                pe_state = {"psT_free": [0, 0]}
                x_free = [0, 0]
                xT_free = 0
                psK_free = [0, 0, 0, 0]
                psV_free = [0, 0]
                xs_at = [0, 0]

                def _p0_stage(bb):
                    par = bb % 2
                    p.wait("gpsimd", "pe", x_free[par])
                    at = 0
                    for tt in range(4):
                        tg = bb * 4 + tt
                        at = p.dma("gpsimd", x_st[par][:, tt * D:(tt + 1) * D],
                                   xf[tg * 128:(tg + 1) * 128, :], f"dx{par}")
                    xs_at[par] = at

                if P0_BLOCKS:
                    _p0_stage(0)
                for blk in range(P0_BLOCKS):
                    par = blk % 2
                    if blk + 1 < P0_BLOCKS:
                        _p0_stage(blk + 1)
                    _emit_cast(blk)
                    p.wait("tensor", f"dx{par}", xs_at[par])
                    p.wait("tensor", "pe", xT_free)
                    for tt in range(4):
                        emit_transpose_tile(p, psT, ident_bf,
                                            x_st[par][:, tt * D:(tt + 1) * D],
                                            xTv, tt, pe_state)
                    x_free[par] = p.cnt["pe"]
                    p.wait("tensor", "se", p.cnt["se"])
                    p.wait("tensor", "dkv", w_dkv)
                    for gi in range(G):
                        kpar = gi % 4
                        p.wait("tensor", "ve", psK_free[kpar])
                        for dc in range(NDC):
                            p.op("tensor",
                                 lambda E, o=psK[kpar][:], l=wkv[:, dc, gi * 128:(gi + 1) * 128],
                                 r=xTv[:, dc, :], s=(dc == 0), t=(dc == NDC - 1):
                                 E.matmul(o, l, r, start=s, stop=t),
                                 inc="pe" if dc == NDC - 1 else None)
                        psK_free[kpar] = emit_rope(
                            p, psK[kpar], csf_sb[:], blk * 512,
                            Kv[0:64, gi, blk * 512:(blk + 1) * 512],
                            Kv[64:128, gi, blk * 512:(blk + 1) * 512],
                            rsc1, rsc2, p.cnt["pe"])
                    for ts4 in range(4):
                        kc = blk * 4 + ts4
                        vpar = ts4 % 2
                        p.wait("tensor", "ve", psV_free[vpar])
                        for dc in range(NDC):
                            p.op("tensor",
                                 lambda E, o=psV[vpar][:], l=xTv[:, dc, ts4 * 128:(ts4 + 1) * 128],
                                 r=wvv[:, dc, :], s=(dc == 0), t=(dc == NDC - 1):
                                 E.matmul(o, l, r, start=s, stop=t),
                                 inc="pe" if dc == NDC - 1 else None)
                        p.wait("vector", "pe", p.cnt["pe"])
                        psV_free[vpar] = p.op("vector",
                                              lambda E, o=Vv[:, kc, :], i=psV[vpar][:]:
                                              E.tensor_copy(o, i), inc="ve")
                    xT_free = p.cnt["pe"]

                p.tap("K_sb", _mk_tap(K_sb), [("ve", p.cnt["ve"])])
                p.tap("V_sb", _mk_tap(V_sb), [("ve", p.cnt["ve"])])
                p.barrier(bar_scratch)

            if phase_limit < 1:
                _emit_dummy_out(p, nc, out_d, bar_scratch)
                p.finalize(e(nc.Block()))
                return nc, p

            E_h = [ek(nc.sbuf_tensor(f"E_h{i}", [128, YT], BF16)) for i in range(4)]
            eh_war = [0, 0, 0, 0]
            eh_at = [0, 0, 0, 0]
            cso_sb = ek(nc.sbuf_tensor("cso_sb", [128, NSLOT * 512], F32))
            p.dma("sync", cso_sb[:], cso_in, "lds")
            p1_lds = p.cnt["lds"]

            x_stage = ek(nc.sbuf_tensor("x_stage", [128, 4 * D], BF16))
            xstage_free = 0

            def _prefetch_xstage(sl):
                p.wait("gpsimd", "pe", xstage_free)
                at = 0
                for tt in range(4):
                    at = p.dma("gpsimd", x_stage[:, tt * D:(tt + 1) * D],
                               xo[(sl * 4 + tt) * 128:(sl * 4 + tt + 1) * 128, :],
                               "dxs")
                return at

            xs_at = _prefetch_xstage(0)

            for slot in range(NSLOT):
                with ExitStack() as s1:
                    e1 = s1.enter_context
                    Q_all = e1(nc.sbuf_tensor(f"Q_all{slot}", [128, H * 512], BF16))
                    attnT = e1(nc.sbuf_tensor(f"attnT{slot}", [128, H * 512], BF16))
                    with ExitStack() as s1b:
                        e1b = s1b.enter_context
                        xT_e = e1b(nc.sbuf_tensor(f"xT_e{slot}", [128, NDC * 512], BF16))
                        wq_sb = [e1b(nc.sbuf_tensor(f"wq_sb{slot}_{i}", [128, NDC * 512], BF16)) for i in range(2)]
                        rsc1 = e1b(nc.sbuf_tensor(f"rsc1b{slot}", [64, 512], F32))
                        rsc2 = e1b(nc.sbuf_tensor(f"rsc2b{slot}", [64, 512], F32))
                        psT = [e1b(nc.psum_tensor(f"psT1{slot}_{i}", [128, 512], BF16)) for i in range(2)]
                        psQ = [e1b(nc.psum_tensor(f"psQ{slot}_{i}", [128, 512], F32)) for i in range(4)]

                        xTv = xT_e[:].rearrange("p (dc t) -> p dc t", t=512)
                        pe_state = {"psT_free": [0, 0]}
                        p.wait("tensor", "dxs", xs_at)
                        for tt in range(4):
                            emit_transpose_tile(p, psT, ident_bf,
                                                x_stage[:, tt * D:(tt + 1) * D],
                                                xTv, tt, pe_state)
                        xstage_free = p.cnt["pe"]
                        p.wait("tensor", "se", p.cnt["se"])
                        if slot + 1 < NSLOT:
                            xs_at = _prefetch_xstage(slot + 1)
                        p.wait("vector", "lds", p1_lds)

                        wq_free = [0, 0]
                        wq_at = [0, 0]
                        psQ_free = [0, 0, 0, 0]

                        def _wq_dma(qtr):
                            par = qtr % 2
                            dstv = wq_sb[par][:].rearrange("p (dc h j) -> p dc h j",
                                                           h=4, j=128)
                            p.wait("gpsimd", "pe", wq_free[par])
                            p.wait("scalar", "pe", wq_free[par])
                            p.dma("gpsimd", dstv[:, 0:8, :, :],
                                  wq_bf[:, 0:8, qtr * 4:(qtr + 1) * 4, :], f"dq{par}")
                            wq_at[par] = p.dma(
                                "scalar", dstv[:, 8:16, :, :],
                                wq_bf[:, 8:16, qtr * 4:(qtr + 1) * 4, :], f"dq{par}")
                        _wq_dma(0)
                        _wq_dma(1)
                        for qtr in range(4):
                            par = qtr % 2
                            p.wait("tensor", f"dq{par}", wq_at[par])
                            wqv = wq_sb[par][:].rearrange("p (dc c) -> p dc c", c=512)
                            for hh in range(4):
                                h = qtr * 4 + hh
                                qpar = h % 4
                                p.wait("tensor", "ve", psQ_free[qpar])
                                for dc in range(NDC):
                                    p.op("tensor",
                                         lambda E, o=psQ[qpar][:], l=wqv[:, dc, hh * 128:(hh + 1) * 128],
                                         r=xTv[:, dc, :], s=(dc == 0), t=(dc == NDC - 1):
                                         E.matmul(o, l, r, start=s, stop=t),
                                         inc="pe" if dc == NDC - 1 else None)
                                psQ_free[qpar] = emit_rope(
                                    p, psQ[qpar], cso_sb[:], slot * 512,
                                    Q_all[0:64, h * 512:(h + 1) * 512],
                                    Q_all[64:128, h * 512:(h + 1) * 512],
                                    rsc1, rsc2, p.cnt["pe"])
                            if qtr < 2:
                                wq_free[par] = p.cnt["pe"]
                                _wq_dma(qtr + 2)
                        p.wait("tensor", "ve", p.cnt["ve"])
                        p.tap(f"Q_{slot}", _mk_tap(Q_all), [("ve", p.cnt["ve"])])

                    NB = 4
                    with ExitStack() as s1c:
                        e1c = s1c.enter_context
                        ew = 1408 + 1024 * slot
                        expt = [e1c(nc.sbuf_tensor(f"expt{slot}_{i}", [128, 512], BF16)) for i in range(NB)]
                        expw = [e1c(nc.sbuf_tensor(f"expw{slot}_{i}", [128, 512], BF16)) for i in range(NB)]
                        rb_sb = e1c(nc.sbuf_tensor(f"rb_sb{slot}", [128, 512], F32))
                        psS = [e1c(nc.psum_tensor(f"psS{slot}_{i}", [128, 512], F32)) for i in range(NB)]
                        psSum = [e1c(nc.psum_tensor(f"psSum{slot}_{i}", [128, 512], F32)) for i in range(2)]
                        psAV = [e1c(nc.psum_tensor(f"psAV{slot}_{i}", [128, 512], F32)) for i in range(2)]

                        nk = 8 * slot + 8
                        LAG = 3
                        norm_done = [p.cnt["ve"], p.cnt["ve"]]
                        pending = [None]

                        def _eh_load(h):
                            epar = h % 4
                            half = (ew // 1024) * 512 + 512
                            p.wait("sync", "ve", eh_war[epar])
                            p.wait("gpsimd", "ve", eh_war[epar])
                            a1 = p.dma("sync", E_h[epar][:, 0:half],
                                       etab[h][:, 0:half], "det")
                            a2 = p.dma("gpsimd", E_h[epar][:, half:ew],
                                       etab[h][:, half:ew], "dtg")
                            eh_at[epar] = (a1, a2)
                        for hh0 in range(4):
                            _eh_load(hh0)

                        def _normalize(h, mm_last):
                            hp = h % 2
                            p.wait("vector", "pe", mm_last)
                            p.op("vector", lambda E, o=rb_sb[:], i=psSum[hp][:]:
                                 E.reciprocal(o, i))
                            nd = p.op(
                                "vector",
                                lambda E, o=attnT[:, h * 512:(h + 1) * 512],
                                a=psAV[hp][:], b=rb_sb[:]:
                                E.tensor_tensor(o, a, b, ALU.mult), inc="ve")
                            norm_done[hp] = nd
                            return nd

                        for h in range(H):
                            gi = h // 4
                            epar = h % 4
                            hp = h % 2
                            p.wait("vector", "det", eh_at[epar][0])
                            p.wait("vector", "dtg", eh_at[epar][1])
                            exp_done = [p.cnt["se"]] * NB
                            tt_done = [p.cnt["ve"]] * NB
                            mm_done = [p.cnt["pe"]] * NB
                            sc_done = [0] * NB
                            for step in range(nk + LAG):
                                kc = step
                                if kc < nk:
                                    bpar = kc % NB
                                    p.wait("tensor", "se", exp_done[bpar])
                                    sc_done[bpar] = p.op(
                                        "tensor",
                                        lambda E, o=psS[bpar][:],
                                        l=Kv[:, gi, kc * 128:(kc + 1) * 128],
                                        r=Q_all[:, h * 512:(h + 1) * 512]:
                                        E.matmul(o, l, r, start=True, stop=True),
                                        inc="pe")
                                if kc == LAG - 1 and pending[0] is not None:
                                    norm_prev = pending[0]
                                    pending[0] = None
                                    norm_prev()
                                j = step - LAG
                                if j < 0:
                                    continue
                                bpar = j % NB
                                p.wait("scalar", "pe", sc_done[bpar])
                                p.wait("scalar", "ve", tt_done[bpar])
                                exp_done[bpar] = p.op_relaxed(
                                    "scalar",
                                    lambda E, o=expt[bpar][:], i=psS[bpar][:]:
                                    E.activation(out=o, in_=i, func=AF.Exp,
                                                 bias=expbias[:], scale=SCALE))
                                y0 = CPRIME + slot * 1024 - j * 128
                                p.wait("vector", "se", exp_done[bpar])
                                p.wait("vector", "pe", mm_done[bpar])
                                tt_done[bpar] = p.op_relaxed(
                                    "vector",
                                    lambda E, o=expw[bpar][:], a=expt[bpar][:],
                                    b=E_h[epar][:, y0:y0 + 512]:
                                    E.tensor_tensor(o, a, b, ALU.mult))
                                p.wait("tensor", "ve", tt_done[bpar])
                                if j == 0:
                                    p.wait("tensor", "ve", norm_done[hp])
                                p.op("tensor",
                                     lambda E, o=psSum[hp][:], l=ones_sq[:], r=expw[bpar][:],
                                     s=(j == 0), t=(j == nk - 1):
                                     E.matmul(o, l, r, start=s, stop=t))
                                mm_done[bpar] = p.op(
                                    "tensor",
                                    lambda E, o=psAV[hp][:], l=Vv[:, j, gi * 128:(gi + 1) * 128],
                                    r=expw[bpar][:], s=(j == 0), t=(j == nk - 1):
                                    E.matmul(o, l, r, start=s, stop=t),
                                    inc="pe")
                            mm_last = mm_done[(nk - 1) % NB]
                            eh_war[epar] = p.cnt["ve"]
                            if h + 4 < H:
                                _eh_load(h + 4)
                            pending[0] = (lambda hh=h, ml=mm_last: _normalize(hh, ml))
                        pending[0]()
                        pending[0] = None
                        p.wait("sync", "ve", p.cnt["ve"])
                        p.dma("sync", attnT_park[slot][:, 0:H * 256],
                              attnT[:, 0:H * 256], "dpark")
                        p.dma("sync", attnT_park[slot][:, H * 256:H * 512],
                              attnT[:, H * 256:H * 512], "dpark")
                        p.tap(f"attnT_{slot}", _mk_tap(attnT), [("ve", p.cnt["ve"])])
                    p.barrier(bar_scratch)

        if phase_limit < 2:
            _emit_dummy_out(p, nc, out_d, bar_scratch)
            p.finalize(e(nc.Block()))
            return nc, p

        TB = 1024
        y1T_blk = e(nc.sbuf_tensor("y1T_blk", [128, NDC * TB], BF16))
        with ExitStack() as s2:
            e2 = s2.enter_context
            wo_sb = e2(nc.sbuf_tensor("wo_sb", [128, NDC * D], BF16))
            attnT_e = [e2(nc.sbuf_tensor(f"attnT_e{i}", [128, H * 512], BF16)) for i in range(2)]
            xres = [e2(nc.sbuf_tensor(f"xres{i}", [128, D], F32)) for i in range(2)]
            y1pre = [e2(nc.sbuf_tensor(f"y1pre{i}", [128, D], F32)) for i in range(2)]
            y1f = [e2(nc.sbuf_tensor(f"y1f{i}", [128, D], F32)) for i in range(2)]
            y1bf = [e2(nc.sbuf_tensor(f"y1bf{i}", [128, D], BF16)) for i in range(2)]
            y1T_e = e2(nc.sbuf_tensor("y1T_e", [128, NDC * 512], BF16))
            st = {
                "sum": e2(nc.sbuf_tensor("st_sum", [128, 1], F32)),
                "sumsq": e2(nc.sbuf_tensor("st_sumsq", [128, 1], F32)),
                "mean": e2(nc.sbuf_tensor("st_mean", [128, 1], F32)),
                "t1": e2(nc.sbuf_tensor("st_t1", [128, 1], F32)),
                "m2": e2(nc.sbuf_tensor("st_m2", [128, 1], F32)),
                "rv": e2(nc.sbuf_tensor("st_rv", [128, 1], F32)),
                "rstd": e2(nc.sbuf_tensor("st_rstd", [128, 1], F32)),
            }
            psY = [e2(nc.psum_tensor(f"psY{i}", [128, 512], F32)) for i in range(4)]
            psT = [e2(nc.psum_tensor(f"psT2{i}", [128, 512], BF16)) for i in range(2)]

            wov = wo_sb[:].rearrange("p (dc c) -> p dc c", c=D)
            for wi in range(8):
                p.dma("gpsimd", wov[:, wi * 2:(wi + 1) * 2, :],
                      wo_bf[:, wi * 2:(wi + 1) * 2, :], "ldg")
            wo_ldg = p.cnt["ldg"]
            xres_free = [0, 0]
            psY_free = 0
            y1T_parked = [0]
            y1_park_at = [0, 0]
            y1_tr_done = [0, 0]
            pe_state = {"psT_free": [0, 0]}
            aT_at = [0, 0]
            aT_free = [0, 0]

            def _at_load(slot):
                par = slot % 2
                p.wait("gpsimd", "pe", aT_free[par])
                at = 0
                for st in range(8):
                    at = p.dma("gpsimd",
                               attnT_e[par][:, st * 1024:(st + 1) * 1024],
                               attnT_park[slot][:, st * 1024:(st + 1) * 1024],
                               "dat")
                aT_at[par] = at
            pending_tr = [None]

            def _flush_tr():
                if pending_tr[0] is not None:
                    fn = pending_tr[0]
                    pending_tr[0] = None
                    fn()

            _at_load(0)
            for slot in range(NSLOT):
                par2 = slot % 2
                if slot + 1 < NSLOT:
                    _at_load(slot + 1)
                aTv = attnT_e[par2][:].rearrange("p (dc t) -> p dc t", t=512)
                if slot < 2:
                    y1Tv = (y1T_blk[:].rearrange("p (dc t) -> p dc t", t=TB)
                            [:, :, slot * 512:(slot + 1) * 512])
                else:
                    y1Tv = y1T_e[:].rearrange("p (dc t) -> p dc t", t=512)
                p.wait("tensor", "dat", aT_at[par2])
                p.wait("tensor", "ldg", wo_ldg)
                for ts in range(4):
                    par = ts % 2
                    p.wait("sync", "ve", xres_free[par])
                    xr_at = p.dma("sync", xres[par][:],
                                  xo[(slot * 4 + ts) * 128:(slot * 4 + ts + 1) * 128, :],
                                  f"dxr{par}")
                    p.wait("tensor", "ve", psY_free)
                    for dc in range(NDC):
                        for do in range(4):
                            p.op("tensor",
                                 lambda E, o=psY[do][:], l=aTv[:, dc, ts * 128:(ts + 1) * 128],
                                 r=wov[:, dc, do * 512:(do + 1) * 512],
                                 s=(dc == 0), t=(dc == NDC - 1):
                                 E.matmul(o, l, r, start=s, stop=t),
                                 inc="pe" if (dc == NDC - 1 and do == 3) else None)
                    mmc = p.cnt["pe"]
                    _flush_tr()
                    p.wait("vector", "pe", mmc)
                    p.wait("vector", f"dxr{par}", xr_at)
                    p.wait("vector", f"dy{par}", y1_park_at[par])
                    p.wait("vector", "pe", y1_tr_done[par])
                    for do in range(4):
                        p.op("vector",
                             lambda E, o=y1pre[par][:, do * 512:(do + 1) * 512],
                             a=psY[do][:],
                             b=xres[par][:, do * 512:(do + 1) * 512]:
                             E.tensor_tensor(o, a, b, ALU.add),
                             inc="ve" if do == 3 else None)
                    psY_free = p.cnt["ve"]
                    xres_free[par] = psY_free
                    c = emit_ln_stats_normalize(p, y1pre[par][:], y1f[par][:], st,
                                                1.0 / D, junk=y1f[par][:])
                    cbf = p.op("vector", lambda E, o=y1bf[par][:], i=y1f[par][:]:
                               E.tensor_copy(o, i), inc="ve")
                    p.wait("scalar", "ve", c)
                    y1_park_at[par] = p.dma(
                        "scalar", y1_park[(slot * 4 + ts) * 128:(slot * 4 + ts + 1) * 128, :],
                        y1f[par][:], f"dy{par}")

                    def _tr(par=par, ts=ts, sl=slot, cbf=cbf, yv=y1Tv):
                        p.wait("tensor", "ve", cbf)
                        if sl >= 2:
                            p.wait("tensor", "dy1t", y1T_parked[0])
                        emit_transpose_tile(p, psT, ident_bf, y1bf[par], yv, ts,
                                            pe_state)
                        y1_tr_done[par] = p.cnt["pe"]
                        if ts == 3 and sl >= 2:
                            p.wait("gpsimd", "se", p.cnt["se"])
                            p.wait("gpsimd", "pe", p.cnt["pe"])
                            pv = y1T_park[sl].rearrange("p (dc t) -> p dc t", t=512)
                            at = 0
                            for q4 in range(4):
                                at = p.dma("gpsimd", pv[:, q4 * 4:(q4 + 1) * 4, :],
                                           yv[:, q4 * 4:(q4 + 1) * 4, :], "dy1t")
                            y1T_parked[0] = at
                    pending_tr[0] = _tr
                aT_free[par2] = p.cnt["pe"]
            _flush_tr()
            p.barrier(bar_scratch)

        if phase_limit < 3:
            _emit_dummy_out(p, nc, out_d, bar_scratch)
            p.finalize(e(nc.Block()))
            return nc, p

        with ExitStack() as s3:
            e3 = s3.enter_context
            y_acc = e3(nc.sbuf_tensor("y_acc", [128, 8 * D], F32))
            h_sb = e3(nc.sbuf_tensor("h_sb", [128, HC * TB], BF16))
            wd_sb = [e3(nc.sbuf_tensor(f"wd_sb{i}", [128, HC * 1024], BF16)) for i in range(2)]
            wgc = [e3(nc.sbuf_tensor(f"wgc{i}", [128, NDC * 128], BF16)) for i in range(4)]
            wuc = [e3(nc.sbuf_tensor(f"wuc{i}", [128, NDC * 128], BF16)) for i in range(4)]
            sg_half = [e3(nc.sbuf_tensor(f"sgh{i}", [128, 512], BF16)) for i in range(2)]
            ctok = e3(nc.sbuf_tensor("ctok", [128, 64], F32))
            gat = {
                "eu": e3(nc.sbuf_tensor("g_eu", [128, 8], F32)),
                "gsum": e3(nc.sbuf_tensor("g_gsum", [128, 1], F32)),
                "rg": e3(nc.sbuf_tensor("g_rg", [128, 1], F32)),
                "gates": e3(nc.sbuf_tensor("g_gates", [128, 8], F32)),
                "m1": e3(nc.sbuf_tensor("g_m1", [128, 1], F32)),
                "mlt": e3(nc.sbuf_tensor("g_mlt", [128, 8], F32)),
                "g2": e3(nc.sbuf_tensor("g_g2", [128, 8], F32)),
                "m2": e3(nc.sbuf_tensor("g_m2", [128, 1], F32)),
                "keep": e3(nc.sbuf_tensor("g_keep", [128, 8], F32)),
                "den": e3(nc.sbuf_tensor("g_den", [128, 1], F32)),
                "rden": e3(nc.sbuf_tensor("g_rden", [128, 1], F32)),
            }
            st8 = {k: e3(nc.sbuf_tensor(f"s3_{k}", [128, 8], F32))
                   for k in ("sum", "sumsq", "mean", "t1", "m2", "rv", "rstd")}
            y2junk = e3(nc.sbuf_tensor("y2junk", [128, D], F32))

            y1Tb = y1T_blk[:].rearrange("p (dc t) -> p dc t", t=TB)
            yav = y_acc[:].rearrange("p (s d) -> p s d", d=D)
            hv = h_sb[:].rearrange("p (hc t) -> p hc t", t=TB)
            wdv = [wd_sb[i][:].rearrange("p (hc d) -> p hc d", d=1024) for i in range(2)]
            rwv = rw_sb[:].rearrange("p (dc e) -> p dc e", e=8)


            psR = [e3(nc.psum_tensor(f"psR{i}", [128, 8], F32)) for i in range(1)]
            psG = [e3(nc.psum_tensor(f"psG{i}", [128, 512], F32)) for i in range(2)]
            psU = [e3(nc.psum_tensor(f"psU{i}", [128, 512], F32)) for i in range(2)]
            psW = [e3(nc.psum_tensor(f"psW{i}", [128, 512], F32)) for i in range(3)]
            wgc_free = [p.cnt["pe"]] * 4
            wuc_free = [p.cnt["pe"]] * 4
            wd_free = [p.cnt["pe"], p.cnt["pe"]]
            hs_free = [p.cnt["pe"]] * HC
            psR_free = [p.cnt["se"]]
            psG_free = [p.cnt["se"], p.cnt["se"]]
            psU_free = [p.cnt["ve"], p.cnt["ve"]]
            psW_free = [p.cnt["ve"]] * 3
            ctsb_done = 0
            yadd_last = 0
            gu_last_pe = 0
            y1t_at = [0, 0]
            yacc_at = [0, 0]
            NUNIT = NE + 2

            def _unit_srcs(unit):
                if unit < NE:
                    return (wg_p[unit], wu_p[unit],
                            wd_r[unit].rearrange("hc p d -> p hc d"))
                sh = unit - NE
                return (sg_p[sh * HC:(sh + 1) * HC], su_p[sh * HC:(sh + 1) * HC],
                        sd_r[sh * HC:(sh + 1) * HC].rearrange("hc p d -> p hc d"))

            NCHUNK = 2 * NUNIT * HC
            gu_at = {}

            def _issue_gu(gidx):
                if gidx >= NCHUNK:
                    return
                u, hc = divmod(gidx % (NUNIT * HC), HC)
                par = gidx % 4
                wg_src, wu_src, _ = _unit_srcs(u)
                p.wait("gpsimd", "pe", wgc_free[par])
                ga = p.dma("gpsimd",
                           wgc[par][:].rearrange("p (dc j) -> p dc j", j=128),
                           wg_src[hc], f"dg{par % 2}")
                p.wait("gpsimd", "pe", wuc_free[par])
                ua = p.dma("gpsimd",
                           wuc[par][:].rearrange("p (dc j) -> p dc j", j=128),
                           wu_src[hc], f"du{par % 2}")
                gu_at[gidx] = (par, ga, ua)

            def _y1t_load(bb, engs, pe_gate):
                for eng in engs:
                    p.wait(eng, "pe", pe_gate)
                at = 0
                k = 0
                for half in range(2):
                    slot = bb * 2 + half
                    dst = (y1T_blk[:].rearrange("p (dc t) -> p dc t", t=TB)
                           [:, :, half * 512:(half + 1) * 512])
                    src = y1T_park[slot].rearrange("p (dc t) -> p dc t", t=512)
                    for q4 in range(4):
                        at = p.dma(engs[k % len(engs)],
                                   dst[:, q4 * 4:(q4 + 1) * 4, :],
                                   src[:, q4 * 4:(q4 + 1) * 4, :], "lds")
                        k += 1
                y1t_at[bb] = at

            def _yacc_load(bb, engs):
                for eng in engs:
                    p.wait(eng, "ve", p.cnt["ve"])
                    p.wait(eng, "po", p.cnt["po"])
                    p.wait(eng, "do0", p.cnt["do0"])
                    p.wait(eng, "do1", p.cnt["do1"])
                at = 0
                for st in range(4):
                    at = p.dma(engs[st % len(engs)], yav[:, st * 2:(st + 1) * 2, :],
                               y1_park[bb * TB + st * 256:bb * TB + (st + 1) * 256, :]
                               .rearrange("(s p) d -> p s d", p=128), "lds")
                yacc_at[bb] = at

            y1t_at[0] = 0
            _yacc_load(0, ("sync", "scalar"))
            for g0 in range(4):
                _issue_gu(g0)

            ctsb_hold = [0]
            pending_norm = []

            def _emit_router_ts(blkb, ts):
                    rp = 0
                    p.wait("tensor", "se", psR_free[rp])
                    for dc in range(NDC):
                        p.op("tensor",
                             lambda E, o=psR[rp][:], l=y1Tb[:, dc, ts * 128:(ts + 1) * 128],
                             r=rwv[:, dc, :], s=(dc == 0), t=(dc == NDC - 1):
                             E.matmul(o, l, r, start=s, stop=t),
                             inc="pe" if dc == NDC - 1 else None)
                    rmm = p.cnt["pe"]
                    p.wait("scalar", "pe", rmm)
                    p.wait("scalar", "ve", ctsb_hold[0])
                    c = p.op("scalar", lambda E, o=gat["eu"][:], i=psR[rp][:]:
                             E.activation(out=o, in_=i, func=AF.Exp), inc="se")
                    psR_free[rp] = c
                    p.wait("vector", "se", c)
                    p.op("vector", lambda E, o=gat["gsum"][:], i=gat["eu"][:]:
                         E.reduce_sum(o, i, axis=AX.X))
                    p.op("vector", lambda E, o=gat["rg"][:], i=gat["gsum"][:]:
                         E.reciprocal(o, i))
                    p.op("vector", lambda E, o=gat["gates"][:], i=gat["eu"][:],
                         s1=gat["rg"][:]:
                         E.tensor_scalar(o, i, s1, None, ALU.mult))
                    p.op("vector", lambda E, o=gat["m1"][:], i=gat["gates"][:]:
                         E.reduce_max(o, i, axis=AX.X))
                    p.op("vector", lambda E, o=gat["mlt"][:], i=gat["gates"][:],
                         s1=gat["m1"][:]:
                         E.tensor_scalar(o, i, s1, None, ALU.is_lt))
                    p.op("vector", lambda E, o=gat["g2"][:], a=gat["gates"][:],
                         b=gat["mlt"][:]:
                         E.tensor_tensor(o, a, b, ALU.mult))
                    p.op("vector", lambda E, o=gat["m2"][:], i=gat["g2"][:]:
                         E.reduce_max(o, i, axis=AX.X))
                    p.op("vector", lambda E, o=gat["keep"][:], i=gat["gates"][:],
                         s1=gat["m2"][:]:
                         E.tensor_scalar(o, i, s1, None, ALU.is_ge))
                    p.op("vector", lambda E, o=gat["den"][:], i=gat["m1"][:],
                         s1=gat["m2"][:]:
                         E.tensor_scalar(o, i, s1, EPS_GATE, ALU.add, ALU.add))
                    p.op("vector", lambda E, o=gat["rden"][:], i=gat["den"][:]:
                         E.reciprocal(o, i))
                    ctsb_hold[0] = p.op(
                        "vector",
                        lambda E, o=ctok[:, ts * 8:(ts + 1) * 8], i=gat["gates"][:],
                        s1=gat["rden"][:], b=gat["keep"][:]:
                        E.scalar_tensor_tensor(o, i, s1, b, ALU.mult, ALU.mult),
                        inc="ve")


            for blkb in range(2):
                with ExitStack() as s3p:
                    e3p = s3p.enter_context
                    p.wait("tensor", "lds", y1t_at[blkb])
                    p.wait("tensor", "ldg", p.cnt["ldg"])

                    for unit in range(NUNIT):
                        routed = unit < NE
                        _, _, wd_src = _unit_srcs(unit)
                        wd_at = [0, 0]
                        for dhh in range(2):
                            p.wait("gpsimd", "pe", wd_free[dhh])
                            p.dma("gpsimd", wdv[dhh][:, 0:6, :],
                                  wd_src[:, 0:6, dhh * 1024:(dhh + 1) * 1024],
                                  f"dwd{dhh}")
                            wd_at[dhh] = p.dma("gpsimd", wdv[dhh][:, 6:HC, :],
                                               wd_src[:, 6:HC, dhh * 1024:(dhh + 1) * 1024],
                                               f"dwd{dhh}")
                        for hc in range(HC):
                            gidx = (blkb * NUNIT + unit) * HC + hc
                            par, ga, ua = gu_at.pop(gidx)
                            wgv = wgc[par][:].rearrange("p (dc j) -> p dc j", j=128)
                            wuv = wuc[par][:].rearrange("p (dc j) -> p dc j", j=128)
                            p.wait("tensor", f"dg{par % 2}", ga)
                            p.wait("tensor", f"du{par % 2}", ua)
                            for th in range(2):
                                tpar = th
                                p.wait("tensor", "se", psG_free[tpar])
                                for dc in range(NDC):
                                    p.op("tensor",
                                         lambda E, o=psG[tpar][:], l=wgv[:, dc, :],
                                         r=y1Tb[:, dc, th * 512:(th + 1) * 512],
                                         s=(dc == 0), t=(dc == NDC - 1):
                                         E.matmul(o, l, r, start=s, stop=t),
                                         inc="pe" if dc == NDC - 1 else None)
                                gmm = p.cnt["pe"]
                                p.wait("tensor", "ve", psU_free[tpar])
                                for dc in range(NDC):
                                    p.op("tensor",
                                         lambda E, o=psU[tpar][:], l=wuv[:, dc, :],
                                         r=y1Tb[:, dc, th * 512:(th + 1) * 512],
                                         s=(dc == 0), t=(dc == NDC - 1):
                                         E.matmul(o, l, r, start=s, stop=t),
                                         inc="pe" if dc == NDC - 1 else None)
                                umm = p.cnt["pe"]
                                p.wait("scalar", "pe", gmm)
                                p.wait("scalar", "ve", psU_free[tpar])
                                psG_free[tpar] = p.op(
                                    "scalar",
                                    lambda E, o=sg_half[tpar][:], i=psG[tpar][:]:
                                    E.activation(out=o, in_=i, func=AF.Silu), inc="se")
                                p.wait("vector", "pe", hs_free[hc])
                                p.wait("vector", "se", psG_free[tpar])
                                p.wait("vector", "pe", umm)
                                psU_free[tpar] = p.op(
                                    "vector",
                                    lambda E, o=hv[:, hc, th * 512:(th + 1) * 512],
                                    a=psU[tpar][:], b=sg_half[tpar][:]:
                                    E.tensor_tensor(o, a, b, ALU.mult), inc="ve")
                            wgc_free[par] = p.cnt["pe"]
                            wuc_free[par] = p.cnt["pe"]
                            _issue_gu(gidx + 4)
                            if unit == 0 and hc < 8:
                                _emit_router_ts(blkb, hc)
                                if pending_norm:
                                    pending_norm.pop(0)()
                        gu_last_pe = p.cnt["pe"]
                        h_done = p.cnt["ve"]
                        p.wait("vector", "lds", yacc_at[blkb])
                        p.wait("tensor", "ve", h_done)
                        for dhh in range(2):
                            p.wait("tensor", f"dwd{dhh}", wd_at[dhh])
                            for ts in range(8):
                                for do2 in range(2):
                                    do = dhh * 2 + do2
                                    wpar = (ts * 2 + do2) % 3
                                    p.wait("tensor", "ve", psW_free[wpar])
                                    for hc in range(HC):
                                        p.op("tensor",
                                             lambda E, o=psW[wpar][:],
                                             l=hv[:, hc, ts * 128:(ts + 1) * 128],
                                             r=wdv[dhh][:, hc, do2 * 512:(do2 + 1) * 512],
                                             s=(hc == 0), t=(hc == HC - 1):
                                             E.matmul(o, l, r, start=s, stop=t),
                                             inc="pe" if hc == HC - 1 else None)
                                    wmm = p.cnt["pe"]
                                    p.wait("vector", "pe", wmm)
                                    if routed:
                                        p.wait("vector", "ve", ctsb_hold[0])
                                        psW_free[wpar] = p.op(
                                            "vector",
                                            lambda E, o=yav[:, ts, do * 512:(do + 1) * 512],
                                            i=psW[wpar][:],
                                            sc=ctok[:, ts * 8 + unit:ts * 8 + unit + 1],
                                            b=yav[:, ts, do * 512:(do + 1) * 512]:
                                            E.scalar_tensor_tensor(o, i, sc, b,
                                                                   ALU.mult, ALU.add),
                                            inc="ve")
                                    else:
                                        psW_free[wpar] = p.op(
                                            "vector",
                                            lambda E, o=yav[:, ts, do * 512:(do + 1) * 512],
                                            a=yav[:, ts, do * 512:(do + 1) * 512],
                                            b=psW[wpar][:]:
                                            E.tensor_tensor(o, a, b, ALU.add), inc="ve")
                                    if unit == NUNIT - 1 and dhh == 1 and do2 == 1:
                                        p.op("vector",
                                             lambda E, o=st8["sum"][:, ts:ts + 1],
                                             i=yav[:, ts, :]:
                                             E.reduce_sum(o, i, axis=AX.X))
                                        p.op("vector", lambda E, o=y2junk[:],
                                             i=yav[:, ts, :],
                                             a=st8["sumsq"][:, ts:ts + 1]:
                                             E.scalar_tensor_tensor(
                                                 o, i, 1.0, i, ALU.mult, ALU.mult,
                                                 accum_out=a))
                            wd_free[dhh] = p.cnt["pe"]
                        yadd_last = p.cnt["ve"]
                        for hc in range(HC):
                            hs_free[hc] = p.cnt["pe"]
                    if blkb == 0:
                        _y1t_load(1, ("sync",), gu_last_pe)
                    p.op("vector", lambda E, o=st8["mean"][:], i=st8["sum"][:]:
                         E.tensor_scalar(o, i, 1.0 / D, None, ALU.mult))
                    p.op("vector", lambda E, o=st8["t1"][:], i=st8["sumsq"][:]:
                         E.tensor_scalar(o, i, 1.0 / D, None, ALU.mult))
                    p.op("vector", lambda E, o=st8["m2"][:], a=st8["mean"][:]:
                         E.tensor_tensor(o, a, a, ALU.mult))
                    p.op("vector", lambda E, o=st8["t1"][:], a=st8["t1"][:],
                         b=st8["m2"][:]: E.tensor_tensor(o, a, b, ALU.subtract))
                    p.op("vector", lambda E, o=st8["t1"][:], i=st8["t1"][:]:
                         E.tensor_scalar(o, i, EPS_LN, None, ALU.add))
                    cstat = p.op("vector", lambda E, o=st8["rv"][:], i=st8["t1"][:]:
                                 E.reciprocal(o, i), inc="ve")
                    p.wait("scalar", "ve", cstat)
                    csq = p.op("scalar", lambda E, o=st8["rstd"][:], i=st8["rv"][:]:
                               E.activation(out=o, in_=i, func=AF.Sqrt), inc="se")

                    def _norm_store(ts, bb=blkb, csq_=csq):
                        par = ts % 2
                        p.wait("vector", "se", csq_)
                        c = p.op("vector",
                                 lambda E, o=yav[:, ts, :], i=yav[:, ts, :],
                                 m=st8["mean"][:, ts:ts + 1], r=st8["rstd"][:, ts:ts + 1]:
                                 E.tensor_scalar(o, i, m, r, ALU.subtract, ALU.mult),
                                 inc="ve")
                        p.wait("sync", "ve", c)
                        p.dma(
                            "sync",
                            out_d[(bb * 8 + ts) * 128:(bb * 8 + ts + 1) * 128, :],
                            yav[:, ts, :], f"do{par}")
                        if bb == 0 and ts % 2 == 1:
                            st = ts // 2
                            p.wait("sync", "do0", p.cnt["do0"])
                            p.wait("sync", "do1", p.cnt["do1"])
                            yacc_at[1] = p.dma(
                                "sync", yav[:, st * 2:(st + 1) * 2, :],
                                y1_park[TB + st * 256:TB + (st + 1) * 256, :]
                                .rearrange("(s p) d -> p s d", p=128), "lds")
                    if blkb == 0:
                        pending_norm[:] = [lambda ts=ts, f=_norm_store: f(ts)
                                           for ts in range(8)]
                    else:
                        for ts in range(8):
                            _norm_store(ts)
            p.barrier(bar_scratch)

        p.finalize(e(nc.Block()))
        return nc, p


def _emit_dummy_out(p, nc, out_d, bar_scratch):
    z = nc.sbuf_tensor("zline", [128, D], F32)
    with z as zt:
        p.op("vector", lambda E: E.memset(zt[:], 0.0), inc="ve")
        p.wait("sync", "ve", p.cnt["ve"])
        for i in range(NSLOT * 512 // 128):
            p.dma("sync", out_d[i * 128:(i + 1) * 128, :], zt[:], "dpark")
        p.wait("sync", "dpark", p.cnt["dpark"])
        p.barrier(bar_scratch)



def _host_tables():
    half = DH // 2
    theta = 1.0 / ROPE_BASE ** (np.arange(half, dtype=np.float64) / half)
    pos = np.arange(T, dtype=np.float64)
    fr = np.outer(theta, pos)
    csf = np.concatenate([np.cos(fr), np.sin(fr)], axis=0).astype(np.float32)
    ident = np.eye(128, dtype=np.float32)
    return csf, ident


def _etab_for_role(r):
    p = np.arange(128, dtype=np.int64)[:, None]
    y = np.arange(YT, dtype=np.int64)[None, :]
    v = (p - y + (CPRIME - r * 512)).astype(np.float64)
    slopes = 2.0 ** (-8.0 * np.arange(1, H + 1, dtype=np.float64) / H)
    out = np.zeros((H, 128, YT), dtype=ml_dtypes.bfloat16)
    with np.errstate(under="ignore"):
        for h in range(H):
            e = np.where(v <= 0, np.exp(slopes[h] * np.minimum(v, 0.0)), 0.0)
            out[h] = e.astype(ml_dtypes.bfloat16)
    return out


def kernel(L0_attn_x, L0_attn_Wq, L0_attn_Wk, L0_attn_Wv, L0_attn_Wo,
           L0_ffn_router_w, L0_ffn_wg, L0_ffn_wu, L0_ffn_wd,
           L0_ffn_sg, L0_ffn_su, L0_ffn_sd):
    x = np.ascontiguousarray(L0_attn_x, dtype=np.float32)

    wq_p = np.ascontiguousarray(
        L0_attn_Wq.reshape(NDC, 128, 16, 128).transpose(1, 0, 2, 3))
    wk_p = np.ascontiguousarray(L0_attn_Wk.reshape(NDC, 128, 512).transpose(1, 0, 2))
    wv_p = np.ascontiguousarray(L0_attn_Wv.reshape(NDC, 128, 512).transpose(1, 0, 2))
    wo_p = np.ascontiguousarray(L0_attn_Wo.reshape(NDC, 128, D).transpose(1, 0, 2))
    rw_p = np.ascontiguousarray(L0_ffn_router_w.T.reshape(NDC, 128, 8).transpose(1, 0, 2))
    wg_p = np.ascontiguousarray(
        L0_ffn_wg.reshape(NE, NDC, 128, HC, 128).transpose(0, 3, 2, 1, 4))
    wu_p = np.ascontiguousarray(
        L0_ffn_wu.reshape(NE, NDC, 128, HC, 128).transpose(0, 3, 2, 1, 4))
    wd_r = np.ascontiguousarray(L0_ffn_wd.reshape(NE, HC, 128, D))
    sg_p = np.ascontiguousarray(
        L0_ffn_sg.reshape(NDC, 128, 2 * HC, 128).transpose(2, 1, 0, 3))
    su_p = np.ascontiguousarray(
        L0_ffn_su.reshape(NDC, 128, 2 * HC, 128).transpose(2, 1, 0, 3))
    sd_r = np.ascontiguousarray(L0_ffn_sd.reshape(2 * HC, 128, D))

    csf, ident = _host_tables()
    etab_role = [_etab_for_role(0), _etab_for_role(1)]

    nc, _ = build_program()

    in_maps = []
    own_rows = {}
    for c in range(8):
        b, r = c // 2, c % 2
        slots = [2 * i + r for i in range(NSLOT)]
        own_rows[c] = slots
        rows = np.concatenate([np.arange(s * 512, (s + 1) * 512) for s in slots])
        xo = np.ascontiguousarray(x[b][rows])
        cso = np.ascontiguousarray(csf[:, rows])
        in_maps.append({
            "xf": x[b], "xo": xo,
            "wq_p": wq_p, "wk_p": wk_p, "wv_p": wv_p, "wo_p": wo_p, "rw_p": rw_p,
            "wg_p": wg_p, "wu_p": wu_p, "wd_r": wd_r,
            "sg_p": sg_p, "su_p": su_p, "sd_r": sd_r,
            "csf": csf, "cso": cso, "etab": etab_role[r],
            "ident": ident,
        })

    res = run_bass_kernel_spmd(nc, in_maps, core_ids=list(range(8)))

    out = np.empty((B, T, D), dtype=np.float32)
    for c in range(8):
        b = c // 2
        oc = np.asarray(res.results[c]["out"])
        for i, s in enumerate(own_rows[c]):
            out[b, s * 512:(s + 1) * 512, :] = oc[i * 512:(i + 1) * 512, :]
    return out


